# version 16
# baseline (speedup 1.0000x reference)
import contextlib
import numpy as np

import concourse.bass as bass
import concourse.mybir as mybir
from concourse.bass_utils import run_bass_kernel_spmd

F32 = mybir.dt.float32
BF16 = mybir.dt.bfloat16
I32 = mybir.dt.int32
I8 = mybir.dt.int8
AF = mybir.ActivationFunctionType
ALU = mybir.AluOpType

D, DC = 1024, 8
DI, DIC = 2048, 16
H, HC = 3072, 24
NS = 16
TT = 512
EPS = 1e-5
QS = 1.0 / 127.0
QS2 = QS * QS

MU, MSQ, M2, R0, R1 = range(5)
VAR0, LA = MSQ, M2
DELTA = MU
C_DA, C_BXD, C_HS0, C_HS1, C_DB = 0, 512, 1024, 1536, 1536


class Sem:
    def __init__(self, stack, nc, name):
        self.h = stack.enter_context(nc.semaphore(name))
        self.n = 0

    def inc(self, inst, k=1):
        inst.then_inc(self.h, k)
        self.n += k


def build_program(nt, zigzag=False, debug=False):
    nc = bass.Bass("TRN2", target_bir_lowering=False, detect_race_conditions=False)
    NT = nt * TT

    x_fm = nc.dram_tensor("x_fm", [D, NT], F32, kind="ExternalInput").ap()
    win = nc.dram_tensor("win", [8 * 128, 4096], F32, kind="ExternalInput").ap()
    wd = nc.dram_tensor("wd", [8 * 128, 4096], F32, kind="ExternalInput").ap()
    wout = nc.dram_tensor("wout", [4 * 128, 4096], F32, kind="ExternalInput").ap()
    gus = nc.dram_tensor("gus", [12 * 128, 4096], I8, kind="ExternalInput").ap()
    dns = nc.dram_tensor("dns", [6 * 128, 4096], I8, kind="ExternalInput").ap()
    convd = nc.dram_tensor("convd", [4 * 128, DIC * 128], F32, kind="ExternalInput").ap()
    ddiag = nc.dram_tensor("ddiag", [128, DIC * 128], F32, kind="ExternalInput").ap()
    cpre = nc.dram_tensor("cpre", [NS, DI], F32, kind="ExternalInput").ap()
    bpre = nc.dram_tensor("bpre", [DI, NS], F32, kind="ExternalInput").ap()
    convb = nc.dram_tensor("convb", [128, DIC], F32, kind="ExternalInput").ap()
    bdb = nc.dram_tensor("bdb", [128, DIC], F32, kind="ExternalInput").ap()
    gamv = nc.dram_tensor("gamv", [128, DC], F32, kind="ExternalInput").ap()
    betv = nc.dram_tensor("betv", [128, DC], F32, kind="ExternalInput").ap()
    alog = nc.dram_tensor("alog", [1, NS], F32, kind="ExternalInput").ap()
    cvecd = nc.dram_tensor("cvecd", [128, 6], F32, kind="ExternalInput").ap()
    wseld = nc.dram_tensor("wseld", [128, 2], F32, kind="ExternalInput").ap()
    ch_in = nc.dram_tensor("ch_in", [1, 6144], BF16, kind="Internal").ap()
    ch_out = nc.dram_tensor("ch_out", [2, 6144], BF16, kind="Internal").ap()
    cs_in = nc.dram_tensor("cs_in", [1, NS], F32, kind="Internal").ap()
    cs_out = nc.dram_tensor("cs_out", [2, NS], F32, kind="Internal").ap()
    RG = [[0, 1], [2, 3], [4, 5], [6, 7]]
    out_fm = nc.dram_tensor("out_fm", [D, NT], F32, kind="ExternalOutput").ap()
    dbg = {}
    if debug:
        for nm, shp, dt in [("d_r", [128, DC * TT], F32), ("d_xn", [128, DC * TT], F32),
                            ("d_hbf", [128, DC * TT], BF16), ("d_xnbf", [128, DC * TT], BF16),
                            ("d_xspre", [128, DIC * 516], BF16),
                            ("d_gs", [128, DIC * TT], F32), ("d_spy", [128, DIC * TT], BF16),
                            ("d_hh", [128, HC * TT], BF16), ("d_sc", [16, 2576], F32),
                            ("d_sta", [1, 5 * TT], F32)]:
            dbg[nm] = nc.dram_tensor(nm, shp, dt, kind="ExternalOutput").ap()

    s_win = nc.dram_tensor("s_win", [8 * 128, 4096], BF16, kind="Internal").ap()
    s_conv = nc.dram_tensor("s_conv", [4 * 128, 2048], BF16, kind="Internal").ap()
    s_wd = nc.dram_tensor("s_wd", [8 * 128, 4096], BF16, kind="Internal").ap()
    s_wo = nc.dram_tensor("s_wo", [4 * 128, 4096], BF16, kind="Internal").ap()
    s_gu = nc.dram_tensor("s_gu", [12 * 128, 4096], BF16, kind="Internal").ap()
    s_dn = nc.dram_tensor("s_dn", [6 * 128, 4096], BF16, kind="Internal").ap()

    stck = contextlib.ExitStack()
    with stck:
        XR = stck.enter_context(nc.sbuf_tensor("XR", [128, DC * TT], F32))
        TB = stck.enter_context(nc.sbuf_tensor("TB", [128, DC * TT], F32))
        XNBF = stck.enter_context(nc.sbuf_tensor("XNBF", [128, DC * TT], BF16))
        HBF = stck.enter_context(nc.sbuf_tensor("HBF", [128, DC * TT], BF16))
        XSPRE = stck.enter_context(nc.sbuf_tensor("XSPRE", [128, DIC * 516], BF16))
        HH = stck.enter_context(nc.sbuf_tensor("HH", [128, HC * TT], BF16))
        GS = stck.enter_context(nc.sbuf_tensor("GS", [128, DIC * TT], F32))
        SPY = stck.enter_context(nc.sbuf_tensor("SPY", [128, DIC * TT], BF16))
        XO = stck.enter_context(nc.sbuf_tensor("XO", [128, DIC * TT], BF16))
        WP = [stck.enter_context(nc.sbuf_tensor(f"WP{i}", [128, 4096], BF16)) for i in range(3)]
        SC = stck.enter_context(nc.sbuf_tensor("SC", [16, 2576], F32))
        HPK = stck.enter_context(nc.sbuf_tensor("HPK", [128, 48], BF16))
        HIN = stck.enter_context(nc.sbuf_tensor("HIN", [128, 96], BF16))
        HPR = stck.enter_context(nc.sbuf_tensor("HPR", [128, 48], BF16))
        HTMP = stck.enter_context(nc.sbuf_tensor("HTMP", [128, 48], BF16))
        WSEL = stck.enter_context(nc.sbuf_tensor("WSEL", [128, 2], F32))
        HSB = stck.enter_context(nc.sbuf_tensor("HSB", [16, 512], BF16))
        STA = stck.enter_context(nc.sbuf_tensor("STA", [1, 5 * TT], F32))
        GAM = stck.enter_context(nc.sbuf_tensor("GAM", [128, DC], F32))
        BET = stck.enter_context(nc.sbuf_tensor("BET", [128, DC], F32))
        CVB = stck.enter_context(nc.sbuf_tensor("CVB", [128, DIC], F32))
        BDBs = stck.enter_context(nc.sbuf_tensor("BDBs", [128, DIC], F32))
        CBF = stck.enter_context(nc.sbuf_tensor("CBF", [16, DI], BF16))
        DBF = stck.enter_context(nc.sbuf_tensor("DBF", [128, DIC * 128], BF16))
        BBF = stck.enter_context(nc.sbuf_tensor("BBF", [128, DIC * NS], BF16))
        AROW = stck.enter_context(nc.sbuf_tensor("AROW", [1, NS], F32))
        ONE1 = stck.enter_context(nc.sbuf_tensor("ONE1", [1, 128], F32))
        ONEB = stck.enter_context(nc.sbuf_tensor("ONEB", [128, 1], BF16))
        ONEF = stck.enter_context(nc.sbuf_tensor("ONEF", [128, 1], F32))
        ON16 = stck.enter_context(nc.sbuf_tensor("ON16", [1, NS], F32))
        CVEC = stck.enter_context(nc.sbuf_tensor("CVEC", [128, 6], F32))
        pgA = stck.enter_context(nc.psum_tensor("pgA", [128, 2048], F32))
        pgB = stck.enter_context(nc.psum_tensor("pgB", [128, 2048], F32))

        s_x = Sem(stck, nc, "s_x")
        s_w = [Sem(stck, nc, "s_w0"), Sem(stck, nc, "s_w1"), Sem(stck, nc, "s_w2")]
        s_t = Sem(stck, nc, "s_t")
        s_v = Sem(stck, nc, "s_v")
        s_s = Sem(stck, nc, "s_s")
        s_o = Sem(stck, nc, "s_o")
        s_cl = [Sem(stck, nc, "s_cl0"), Sem(stck, nc, "s_cl1")]
        s_cs = [Sem(stck, nc, "s_cs0"), Sem(stck, nc, "s_cs1")]
        s_cc = Sem(stck, nc, "s_cc")
        s_k = Sem(stck, nc, "s_k")
        s_hi = Sem(stck, nc, "s_hi")
        s_ho = Sem(stck, nc, "s_ho")
        s_si = Sem(stck, nc, "s_si")
        s_so = Sem(stck, nc, "s_so")
        block = stck.enter_context(nc.Block())

        XOf = XO.ap().bitcast(F32)
        XSPf = XSPRE.ap().bitcast(F32)

        def sta(i):
            return STA[0:1, i * TT:(i + 1) * TT]

        def xsp(c, lo, hi):
            return XSPRE[:, c * 516 + lo:c * 516 + hi]

        class PhaseBuf:
            def __init__(self):
                self.q = {"t": [], "v": [], "s": [], "g": [], "y": []}

            def t(self, f):
                self.q["t"].append(f)

            def v(self, f):
                self.q["v"].append(f)

            def s(self, f):
                self.q["s"].append(f)

            def g(self, f):
                self.q["g"].append(f)

            def y(self, f):
                self.q["y"].append(f)

            def flush(self):
                qs = self.q
                self.q = {"t": [], "v": [], "s": [], "g": [], "y": []}
                if qs["y"]:
                    @block.sync
                    def _(e, fs=qs["y"]):
                        for f in fs:
                            f(e)
                if qs["g"]:
                    @block.gpsimd
                    def _(e, fs=qs["g"]):
                        for f in fs:
                            f(e)
                if qs["t"]:
                    @block.tensor
                    def _(e, fs=qs["t"]):
                        for f in fs:
                            f(e)
                if qs["v"]:
                    @block.vector
                    def _(e, fs=qs["v"]):
                        for f in fs:
                            f(e)
                if qs["s"]:
                    @block.scalar
                    def _(e, fs=qs["s"]):
                        for f in fs:
                            f(e)

        PH = PhaseBuf()

        jobs = []
        winv = s_win.rearrange("(s p) c -> s p c", p=128)
        srcwin = win.rearrange("(s p) c -> s p c", p=128)
        for s in range(8):
            jobs.append((srcwin[s], F32, winv[s], 4096))
        cvv = s_conv.rearrange("(s p) c -> s p c", p=128)
        for w in range(4):
            jobs.append((convd[w * 128:(w + 1) * 128, :], F32, cvv[w], 2048))
        wdv = s_wd.rearrange("(s p) c -> s p c", p=128)
        srcwd = wd.rearrange("(s p) c -> s p c", p=128)
        for s in range(8):
            jobs.append((srcwd[s], F32, wdv[s], 4096))
        guv = s_gu.rearrange("(s p) c -> s p c", p=128)
        srcgu = gus.rearrange("(s p) c -> s p c", p=128)
        for w in range(12):
            jobs.append((srcgu[w], I8, guv[w], 4096))
        wov = s_wo.rearrange("(s p) c -> s p c", p=128)
        srcwo = wout.rearrange("(s p) c -> s p c", p=128)
        for s in range(4):
            jobs.append((srcwo[s], F32, wov[s], 4096))
        dnv = s_dn.rearrange("(s p) c -> s p c", p=128)
        srcdn = dns.rearrange("(s p) c -> s p c", p=128)
        for s in range(6):
            jobs.append((srcdn[s], I8, dnv[s], 4096))

        XOc = XO.ap().bitcast(F32)
        SPYc = SPY.ap().bitcast(F32)

        def _consts(s):
            for dst, src in [(GAM, gamv), (BET, betv), (CVB, convb), (BDBs, bdb)]:
                s.dma_start(out=dst[:], in_=src).then_inc(s_k.h, 16)
            s.dma_start(out=AROW[:], in_=alog).then_inc(s_k.h, 16)
            s.dma_start(out=CVEC[:], in_=cvecd).then_inc(s_k.h, 16)
            s.dma_start(out=WSEL[:], in_=wseld).then_inc(s_k.h, 16)
            s.dma_start(out=XOc[0:16, 0:DI], in_=cpre).then_inc(s_k.h, 16)
            s.dma_start(
                out=SPYc[:, 0:256].rearrange("p (a b) -> p a b", a=DIC),
                in_=bpre.rearrange("(c p) n -> p c n", p=128)).then_inc(s_k.h, 16)
            s.dma_start(out=XOc[:, DI:2 * DI], in_=ddiag).then_inc(s_k.h, 16)
        PH.s(_consts)
        s_k.n += 16 * 10
        consts_lc = s_k.n

        def _arow(s):
            s.wait_ge(s_k.h, consts_lc)
            s.activation(AROW[:], AROW[:], AF.Exp).then_inc(s_s.h, 1)
        PH.s(_arow)
        s_s.n += 1
        arow_sc = s_s.n

        def _cinit(v):
            v.memset(ONE1[:], 1.0)
            v.memset(ONEB[:], 1.0)
            v.memset(ONEF[:], 1.0)
            v.memset(ON16[:], 1.0)
            v.memset(XSPRE[:], 0.0)
            v.memset(SC[:], 0.0)
            v.memset(HPR[:], 0.0)
            v.memset(HIN[:], 0.0)
            v.wait_ge(s_k.h, consts_lc)
            v.wait_ge(s_s.h, arow_sc)
            v.tensor_scalar(AROW[:], AROW[:], -1.0, None, ALU.mult)
            v.tensor_copy(CBF[:], XOc[0:16, 0:DI])
            v.tensor_copy(BBF[:], SPYc[:, 0:256])
            v.tensor_copy(DBF[:], XOc[:, DI:2 * DI]).then_inc(s_v.h, 1)
        PH.v(_cinit)
        s_v.n += 1
        consts_done_v = s_v.n
        cast_counts = []
        store_counts = []
        grp_ready = {}
        _grp_end = {7: "win", 11: "conv", 19: "wd", 31: "gu", 35: "wo", 41: "dn"}
        GSi8 = GS.ap().bitcast(I8)
        for j, (src, dt, dst, wj) in enumerate(jobs):
            par = j % 2
            if dt == I8:
                fstage = GSi8[:, par * 16384:par * 16384 + wj]
            else:
                fstage = GS[:, par * 4096:par * 4096 + wj]
            bstage = HH[:, par * 4096:par * 4096 + wj]
            scl = s_cl[par]
            scs = s_cs[par]

            def _ld(s, src=src, dst_=fstage, j=j, scl=scl):
                if j >= 2:
                    s.wait_ge(s_v.h, cast_counts[j - 2])
                s.dma_start(out=dst_, in_=src).then_inc(scl.h, 16)
            PH.s(_ld)
            scl.n += 16
            lc = scl.n

            def _cast(v, fs=fstage, bs=bstage, j=j, lc=lc, scl=scl, scs=scs):
                v.wait_ge(scl.h, lc)
                if j >= 2:
                    v.wait_ge(scs.h, store_counts[j - 2])
                v.tensor_copy(bs, fs).then_inc(s_v.h, 1)
            PH.v(_cast)
            s_v.n += 1
            cast_counts.append(s_v.n)

            def _st(g, dst=dst, wj=wj, bs=bstage, vc=s_v.n, scs=scs):
                g.wait_ge(s_v.h, vc)
                g.dma_start(out=dst[:, 0:wj], in_=bs[:, 0:wj]).then_inc(scs.h, 16)
            PH.g(_st)
            scs.n += 16
            store_counts.append(scs.n)
            if j in _grp_end:
                grp_ready[_grp_end[j]] = (s_cs[0].n, s_cs[1].n)

        PH.flush()

        grp = [pgA, pgB]
        grp_req = [None, None]
        wave_ctr = [0]

        def next_group():
            gi = wave_ctr[0] % 2
            wave_ctr[0] += 1
            return gi, grp[gi], grp_req[gi]

        slab_ctr = [0]
        slab_free = []

        seen_grp = set()

        def emit_slab(src2, grp, width=4096):
            i = slab_ctr[0]
            slab_ctr[0] += 1
            par = i % 3
            buf = WP[par]
            sw = s_w[par]
            gw = None
            if grp not in seen_grp:
                seen_grp.add(grp)
                gw = grp_ready.get(grp)
            free_at = slab_free[i - 3] if i >= 3 else None

            def _dma(sy, src2=src2, buf=buf, width=width, sw=sw, gw=gw, free_at=free_at):
                if gw is not None:
                    sy.wait_ge(s_cs[0].h, gw[0])
                    sy.wait_ge(s_cs[1].h, gw[1])
                if free_at is not None:
                    sy.wait_ge(s_t.h, free_at)
                sy.dma_start(out=buf[:, 0:width], in_=src2).then_inc(sw.h, 16)
            PH.y(_dma)
            sw.n += 16
            return buf, (sw, sw.n)

        def slab_done():
            slab_free.append(s_t.n)

        FR = {}

        def emit_front(ft, first=False, whh=0):
            t0f = ft * TT

            def _xld(g, t0f=t0f, wcv=prev_conv_t, whp=halo_v if not zigzag else hpk_v_last[0],
                     wc0=(consts_done_v if first else 0)):
                if wc0:
                    g.wait_ge(s_v.h, wc0)
                if wcv:
                    g.wait_ge(s_t.h, wcv)
                if whp:
                    g.wait_ge(s_v.h, whp)
                srcx = x_fm.rearrange("(c p) t -> p c t", p=128)[:, :, t0f:t0f + TT]
                g.dma_start(out=XSPf[:, 0:4096].rearrange("p (c t) -> p c t", c=DC),
                            in_=srcx).then_inc(s_x.h, 16)
            PH.g(_xld)
            s_x.n += 16
            x_ld = s_x.n

            def _xsq(s, x_ld=x_ld, w8=(whh or consts_done_v)):
                s.wait_ge(s_x.h, x_ld)
                s.wait_ge(s_v.h, w8)
                s.activation(TB[:], XSPf[:, 0:4096], AF.Square).then_inc(s_s.h, 1)
            PH.s(_xsq)
            s_s.n += 1
            xsq_s = s_s.n

            gi1, pg1, req1 = next_group()

            def _stats(te, x_ld=x_ld, xsq_s=xsq_s, req=req1, pg=pg1):
                te.wait_ge(s_x.h, x_ld)
                if req is not None:
                    te.wait_ge(req[0].h, req[1])
                for c in range(DC):
                    te.matmul(pg[0:1, 0:512], ONEF[:], XSPf[:, c * TT:(c + 1) * TT],
                              start=(c == 0), stop=(c == DC - 1))
                te.wait_ge(s_s.h, xsq_s)
                i = None
                for c in range(DC):
                    i = te.matmul(pg[0:1, 512:1024], ONEF[:], TB[:, c * TT:(c + 1) * TT],
                                  start=(c == 0), stop=(c == DC - 1))
                i.then_inc(s_t.h, 1)
            PH.t(_stats)
            s_t.n += 1
            stats_t = s_t.n

            def _tv1(v, w=stats_t, pg=pg1):
                v.wait_ge(s_t.h, w)
                v.tensor_scalar(sta(MU), pg[0:1, 0:512], 1.0 / D, None, ALU.mult)
                v.tensor_scalar(sta(MSQ), pg[0:1, 512:1024], 1.0 / D, None, ALU.mult)
                v.tensor_tensor(sta(M2), sta(MU), sta(MU), ALU.mult)
                v.tensor_tensor(sta(VAR0), sta(MSQ), sta(M2), ALU.subtract).then_inc(s_v.h, 1)
            PH.v(_tv1)
            s_v.n += 1
            st_v = s_v.n
            grp_req[gi1] = (s_v, st_v)

            def _ts1(s, w=st_v):
                s.wait_ge(s_v.h, w)
                s.activation(sta(LA), sta(VAR0), AF.Ln, bias=CVEC[0:1, 0:1])
                s.activation(sta(R0), sta(LA), AF.Exp, scale=CVEC[0:1, 1:2])
                s.activation(sta(LA), sta(VAR0), AF.Ln, bias=CVEC[0:1, 5:6], scale=CVEC[0:1, 4:5])
                s.activation(sta(R1), sta(LA), AF.Exp, scale=CVEC[0:1, 1:2]).then_inc(s_s.h, 1)
            PH.s(_ts1)
            s_s.n += 1
            st_s = s_s.n
            FR[ft] = dict(x_ld=x_ld, st_s=st_s)

        def emit_front2(ft, cur_gu_t):
            st_s = FR[ft]["st_s"]
            gi2, pg2, req2 = next_group()

            def _bc(te, w=st_s, req=req2, pg=pg2):
                te.wait_ge(s_s.h, w)
                if req is not None:
                    te.wait_ge(req[0].h, req[1])
                te.matmul(pg[:, 0:512], ONE1[:], sta(MU), start=True, stop=True)
                te.matmul(pg[:, 512:1024], ONE1[:], sta(R0), start=True, stop=True)
                te.matmul(pg[:, 1024:1536], ONE1[:], sta(R1), start=True, stop=True).then_inc(s_t.h, 1)
            PH.t(_bc)
            s_t.n += 1
            bc_t = s_t.n

            def _ln(v, w=bc_t, wgu=cur_gu_t, pg=pg2):
                v.wait_ge(s_t.h, w)
                v.wait_ge(s_t.h, wgu)
                for c in range(DC):
                    v.tensor_tensor(TB[:, c * TT:(c + 1) * TT], XSPf[:, c * TT:(c + 1) * TT],
                                    pg[:, 0:512], ALU.subtract)
                for c in range(DC):
                    v.tensor_tensor(GS[:, 4096 + c * 512:4096 + (c + 1) * 512],
                                    TB[:, c * TT:(c + 1) * TT], pg[:, 1024:1536], ALU.mult)
                for c in range(DC):
                    v.tensor_tensor(TB[:, c * TT:(c + 1) * TT], TB[:, c * TT:(c + 1) * TT],
                                    pg[:, 512:1024], ALU.mult)
                v.tensor_copy(XNBF[:], TB[:]).then_inc(s_v.h, 1)
            PH.v(_ln)
            s_v.n += 1
            ln_v = s_v.n
            grp_req[gi2] = (s_v, ln_v)

            def _hbf(s, w=ln_v, wwin=prev_win_t):
                s.wait_ge(s_v.h, w)
                if wwin:
                    s.wait_ge(s_t.h, wwin)
                i = None
                for c in range(DC):
                    i = s.activation(HBF[:, c * TT:(c + 1) * TT], GS[:, 4096 + c * 512:4096 + (c + 1) * 512],
                                     AF.Identity, bias=BET[:, c:c + 1], scale=GAM[:, c:c + 1])
                i.then_inc(s_s.h, 1)
            PH.s(_hbf)
            s_s.n += 1
            FR[ft]["ln_v"] = ln_v
            FR[ft]["hbf_s"] = s_s.n

        def emit_r(ft):
            def _r(v):
                v.tensor_tensor(XR[:], XSPf[:, 0:4096], TB[:], ALU.add).then_inc(s_v.h, 1)
            PH.v(_r)
            s_v.n += 1
            FR[ft]["r_v"] = s_v.n
            if not zigzag:
                def _hrest(v, z=(ft == 0)):
                    if z:
                        v.memset(HPK[:], 0.0)
                    v.tensor_copy(
                        XSPRE.ap().rearrange("p (c t) -> p c t", c=DIC)[:, :, 0:3],
                        HPK.ap().rearrange("p (c t) -> p c t", c=DIC)).then_inc(s_v.h, 1)
                PH.v(_hrest)
                s_v.n += 1
            if not zigzag:
                FR[ft]["halo_v"] = s_v.n

        hpk_v_last = [0]
        halo_v = 0
        halo_used_v = 0
        prev_ybf_v = 0
        prev_scan2_v = 0
        prev_hpr_v = 0
        prev_conv_t = 0
        prev_win_t = 0
        prev_y_t = 0

        for t in range(nt):
            t0 = t * TT

            if t == 0:
                emit_front(0, first=True)
                emit_front2(0, 0)
                emit_r(0)
            fr = FR[t]
            x_ld = fr["x_ld"]
            ln_v = fr["ln_v"]
            hbf_s = fr["hbf_s"]
            r_v = fr["r_v"]

            win_evac = []
            for mg in range(8):
                buf, wn = emit_slab(winv[mg], "win")
                gi, pg, req = next_group()

                def _mm(te, buf=buf, wn=wn, req=req, pg=pg, w=hbf_s):
                    te.wait_ge(wn[0].h, wn[1])
                    if req is not None:
                        te.wait_ge(req[0].h, req[1])
                    te.wait_ge(s_s.h, w)
                    i = None
                    for ml in range(4):
                        for k in range(8):
                            i = te.matmul(pg[:, ml * 512:(ml + 1) * 512],
                                          buf[:, k * 512 + ml * 128:k * 512 + ml * 128 + 128],
                                          HBF[:, k * TT:(k + 1) * TT],
                                          start=(k == 0), stop=(k == 7))
                    i.then_inc(s_t.h, 1)
                PH.t(_mm)
                s_t.n += 1
                slab_done()
                wv_t = s_t.n

                def _ev(s, pg=pg, mg=mg, w=wv_t, whal=halo_v, wybf=prev_ybf_v, wr=r_v):
                    s.wait_ge(s_t.h, w)
                    i = None
                    if mg < 4:
                        s.wait_ge(s_v.h, wr)
                        if whal:
                            s.wait_ge(s_v.h, whal)
                        for ml in range(4):
                            c = mg * 4 + ml
                            i = s.copy(xsp(c, 3, 515), pg[:, ml * 512:(ml + 1) * 512])
                    else:
                        if mg == 4 and wybf:
                            s.wait_ge(s_v.h, wybf)
                        for ml in range(4):
                            c = (mg - 4) * 4 + ml
                            i = s.activation(GS[:, c * TT:(c + 1) * TT],
                                             pg[:, ml * 512:(ml + 1) * 512], AF.Silu)
                    i.then_inc(s_s.h, 1)
                PH.s(_ev)
                s_s.n += 1
                grp_req[gi] = (s_s, s_s.n)
                win_evac.append(s_s.n)
            prev_win_t_new = s_t.n

            if zigzag:
                def _hpk(v, w=win_evac[3], wcv=prev_conv_t):
                    v.wait_ge(s_s.h, w)
                    if wcv:
                        v.wait_ge(s_t.h, wcv)
                    v.tensor_copy(HPK.ap().rearrange("p (c t) -> p c t", c=DIC),
                                  XSPRE.ap().rearrange("p (c t) -> p c t", c=DIC)[:, :, 512:515]).then_inc(s_v.h, 1)
                PH.v(_hpk)
                s_v.n += 1
                hpk_v = s_v.n
                hpk_v_last[0] = hpk_v

                def _hin(g, wv=hpk_v, wcc=s_cc.n, who=s_ho.n):
                    g.wait_ge(s_v.h, wv)
                    if wcc:
                        g.wait_ge(s_cc.h, wcc)
                    g.dma_start(out=ch_in.rearrange("a (p c) -> a p c", p=128)[0], in_=HPK[:]).then_inc(s_hi.h, 16)
                PH.g(_hin)
                s_hi.n += 16

                def _agh(g, whi=s_hi.n, who=s_ho.n):
                    g.wait_ge(s_hi.h, whi)
                    if who:
                        g.wait_ge(s_ho.h, who)
                    g.collective_compute("AllGather", ALU.bypass, replica_groups=RG,
                                         ins=[ch_in], outs=[ch_out]).then_inc(s_cc.h, 1)
                PH.g(_agh)
                s_cc.n += 1

                def _hout(g, wcc=s_cc.n, wprv=prev_hpr_v):
                    g.wait_ge(s_cc.h, wcc)
                    if wprv:
                        g.wait_ge(s_v.h, wprv)
                    g.dma_start(out=HIN[:, 0:48], in_=ch_out.rearrange("a (p c) -> a p c", p=128)[0]).then_inc(s_ho.h, 16)
                    g.dma_start(out=HIN[:, 48:96], in_=ch_out.rearrange("a (p c) -> a p c", p=128)[1]).then_inc(s_ho.h, 16)
                PH.g(_hout)
                s_ho.n += 32

                def _husev(v, who=s_ho.n, wcv=prev_conv_t, wr=r_v):
                    v.wait_ge(s_ho.h, who)
                    v.wait_ge(s_v.h, wr)
                    if wcv:
                        v.wait_ge(s_t.h, wcv)
                    v.tensor_scalar(HTMP[:], HPR[:], WSEL[:, 0:1], None, ALU.mult)
                    v.scalar_tensor_tensor(
                        XSPRE.ap().rearrange("p (c t) -> p c t", c=DIC)[:, :, 0:3],
                        HIN[:, 0:48].rearrange("p (c t) -> p c t", c=DIC),
                        WSEL[:, 1:2], HTMP.ap().rearrange("p (c t) -> p c t", c=DIC),
                        ALU.mult, ALU.add)
                    v.tensor_copy(HPR[:], HIN[:, 48:96]).then_inc(s_v.h, 1)
                PH.v(_husev)
                s_v.n += 1
                halo_used_v = s_v.n
                prev_hpr_v = halo_used_v

            for w in range(4):
                buf, wn = emit_slab(cvv[w], "conv", width=2048)
                gi, pg, req = next_group()

                def _mm(te, buf=buf, wn=wn, req=req, pg=pg, w=w, wev=win_evac[w],
                        whv=(halo_used_v if zigzag else FR[t].get("halo_v", 0))):
                    te.wait_ge(wn[0].h, wn[1])
                    if req is not None:
                        te.wait_ge(req[0].h, req[1])
                    te.wait_ge(s_s.h, wev)
                    if whv:
                        te.wait_ge(s_v.h, whv)
                    i = None
                    for cl in range(4):
                        c = w * 4 + cl
                        for k in range(4):
                            i = te.matmul(pg[:, cl * 512:(cl + 1) * 512],
                                          buf[:, (cl * 4 + k) * 128:(cl * 4 + k + 1) * 128],
                                          xsp(c, k, k + 512),
                                          start=(k == 0), stop=(k == 3))
                    i.then_inc(s_t.h, 1)
                PH.t(_mm)
                s_t.n += 1
                slab_done()
                wv_t = s_t.n

                def _ev(s, pg=pg, w=w, wv=wv_t, wo=16 * t):
                    s.wait_ge(s_t.h, wv)
                    if w == 0 and wo:
                        s.wait_ge(s_o.h, wo)
                    i = None
                    for cl in range(4):
                        c = w * 4 + cl
                        i = s.activation(XO[:, c * TT:(c + 1) * TT], pg[:, cl * 512:(cl + 1) * 512],
                                         AF.Silu, bias=CVB[:, c:c + 1])
                    i.then_inc(s_s.h, 1)
                PH.s(_ev)
                s_s.n += 1
                grp_req[gi] = (s_s, s_s.n)
            conv_s = s_s.n
            conv_t = s_t.n

            prev_conv_t = conv_t
            if not zigzag:
                def _halo(v, w=conv_t):
                    v.wait_ge(s_t.h, w)
                    v.tensor_copy(
                        HPK.ap().rearrange("p (c t) -> p c t", c=DIC),
                        XSPRE.ap().rearrange("p (c t) -> p c t", c=DIC)[:, :, 512:515]).then_inc(s_v.h, 1)
                PH.v(_halo)
                s_v.n += 1
                hpk_v_last[0] = s_v.n

            for mg in range(4):
                gi, pg, req = next_group()
                for kh in range(2):
                    buf, wn = emit_slab(wdv[mg * 2 + kh], "wd")

                    def _mm(te, buf=buf, wn=wn, req=req, pg=pg, kh=kh, wc=conv_s):
                        te.wait_ge(wn[0].h, wn[1])
                        if kh == 0:
                            if req is not None:
                                te.wait_ge(req[0].h, req[1])
                            te.wait_ge(s_s.h, wc)
                        i = None
                        for ml in range(4):
                            for kl in range(8):
                                k = kh * 8 + kl
                                i = te.matmul(pg[:, ml * 512:(ml + 1) * 512],
                                              buf[:, kl * 512 + ml * 128:kl * 512 + ml * 128 + 128],
                                              XO[:, k * TT:(k + 1) * TT],
                                              start=(k == 0), stop=(k == 15))
                        i.then_inc(s_t.h, 1)
                    PH.t(_mm)
                    s_t.n += 1
                    slab_done()
                wv_t = s_t.n

                def _ev(s, pg=pg, mg=mg, w=wv_t):
                    s.wait_ge(s_t.h, w)
                    for ml in range(4):
                        c = mg * 4 + ml
                        s.activation(pg[:, ml * 512:(ml + 1) * 512], pg[:, ml * 512:(ml + 1) * 512],
                                     AF.Exp, bias=BDBs[:, c:c + 1])
                    i = None
                    for ml in range(4):
                        c = mg * 4 + ml
                        i = s.activation(SPY[:, c * TT:(c + 1) * TT], pg[:, ml * 512:(ml + 1) * 512],
                                         AF.Ln, bias=1.0)
                    i.then_inc(s_s.h, 1)
                PH.s(_ev)
                s_s.n += 1
                grp_req[gi] = (s_s, s_s.n)
            sp_s = s_s.n

            gu_stt = []

            def emit_gu(ws):
                for w in ws:
                    buf, wn = emit_slab(guv[w], "gu")
                    gi, pg, req = next_group()

                    def _mm(te, buf=buf, wn=wn, req=req, w=w, pg=pg, wln=ln_v):
                        te.wait_ge(wn[0].h, wn[1])
                        if req is not None:
                            te.wait_ge(req[0].h, req[1])
                        if w == 0:
                            te.wait_ge(s_v.h, wln)
                        i = None
                        for gl in range(4):
                            for k in range(8):
                                i = te.matmul(pg[:, gl * 512:(gl + 1) * 512],
                                              buf[:, k * 512 + gl * 128:k * 512 + gl * 128 + 128],
                                              XNBF[:, k * TT:(k + 1) * TT],
                                              start=(k == 0), stop=(k == 7))
                        i.then_inc(s_t.h, 1)
                    PH.t(_mm)
                    s_t.n += 1
                    slab_done()
                    wv_t = s_t.n
                    reg = (w % 4) * 1024

                    def _ev(s, pg=pg, wv=wv_t, w=w, reg=reg):
                        s.wait_ge(s_t.h, wv)
                        if w >= 4:
                            s.wait_ge(s_v.h, gu_stt[w - 4])
                        s.activation(TB[:, reg:reg + 512], pg[:, 0:512], AF.Silu, scale=CVEC[:, 3:4])
                        s.activation(TB[:, reg + 512:reg + 1024], pg[:, 512:1024], AF.Silu, scale=CVEC[:, 3:4]).then_inc(s_s.h, 1)
                    PH.s(_ev)
                    s_s.n += 1
                    gsil_s = s_s.n

                    def _hh(v, pg=pg, w=w, wg=gsil_s, reg=reg):
                        v.wait_ge(s_s.h, wg)
                        v.scalar_tensor_tensor(HH[:, (2 * w) * TT:(2 * w + 1) * TT], pg[:, 1024:1536],
                                               QS2, TB[:, reg:reg + 512], ALU.mult, ALU.mult)
                        i = v.scalar_tensor_tensor(HH[:, (2 * w + 1) * TT:(2 * w + 2) * TT], pg[:, 1536:2048],
                                                   QS2, TB[:, reg + 512:reg + 1024], ALU.mult, ALU.mult)
                        i.then_inc(s_v.h, 1)
                    PH.v(_hh)
                    s_v.n += 1
                    gu_stt.append(s_v.n)
                    grp_req[gi] = (s_v, s_v.n)

            emit_gu(range(4))
            gi, pg, req = next_group()

            def _dl(te, req=req, pg=pg, w=sp_s):
                if req is not None:
                    te.wait_ge(req[0].h, req[1])
                te.wait_ge(s_s.h, w)
                i = None
                for c in range(DIC):
                    i = te.matmul(pg[0:1, 0:512], ONEB[:], SPY[:, c * TT:(c + 1) * TT],
                                  start=(c == 0), stop=(c == DIC - 1))
                i.then_inc(s_t.h, 1)
            PH.t(_dl)
            s_t.n += 1
            dl_t = s_t.n

            def _dlv(v, pg=pg, w=dl_t):
                v.wait_ge(s_t.h, w)
                v.tensor_scalar(sta(DELTA), pg[0:1, 0:512], 1.0 / DI, None, ALU.mult).then_inc(s_v.h, 1)
            PH.v(_dlv)
            s_v.n += 1
            dl_v = s_v.n

            def _ssm(te, pg=pg, w=dl_v):
                te.wait_ge(s_v.h, w)
                te.matmul(pg[0:16, 512:1024], AROW[:], sta(DELTA), start=True, stop=True)
                te.matmul(pg[0:16, 1024:1536], ON16[:], sta(DELTA), start=True, stop=True)
                i = None
                for c in range(DIC):
                    i = te.matmul(pg[0:16, 1536:2048], BBF[:, c * NS:(c + 1) * NS],
                                  XO[:, c * TT:(c + 1) * TT], start=(c == 0), stop=(c == DIC - 1))
                i.then_inc(s_t.h, 1)
            PH.t(_ssm)
            s_t.n += 1
            ssm_t = s_t.n

            def _da(s, pg=pg, w=ssm_t):
                s.wait_ge(s_t.h, w)
                s.activation(SC[:, C_DA:C_DA + 512], pg[0:16, 512:1024], AF.Exp).then_inc(s_s.h, 1)
            PH.s(_da)
            s_s.n += 1
            da_s = s_s.n

            if not zigzag:
                hs_cur = C_HS0 if t % 2 == 0 else C_HS1
                hs_prev = C_HS1 if t % 2 == 0 else C_HS0

                def _scan(v, pg=pg, w=da_s, hc=hs_cur, hp=hs_prev, wy=prev_y_t, t=t):
                    v.wait_ge(s_s.h, w)
                    if wy:
                        v.wait_ge(s_t.h, wy)
                    v.tensor_copy(SC[:, C_DB:C_DB + 512], pg[0:16, 1024:1536])
                    v.tensor_tensor(SC[:, C_BXD:C_BXD + 512], pg[0:16, 1536:2048],
                                    SC[:, C_DB:C_DB + 512], ALU.mult)
                    init = 0.0 if t == 0 else SC[:, hp + 511:hp + 512]
                    v.tensor_tensor_scan(SC[:, hc:hc + 512], SC[:, C_DA:C_DA + 512],
                                         SC[:, C_BXD:C_BXD + 512], init, ALU.mult, ALU.add)
                    v.tensor_copy(HSB[:], SC[:, hc:hc + 512]).then_inc(s_v.h, 1)
                PH.v(_scan)
                s_v.n += 1
                scan_v = s_v.n
                grp_req[gi] = (s_v, scan_v)
            else:
                S_SEL1, S_SEL2, S_PREV1, S_CUR0, C_HS2 = 2048, 2049, 2050, 2051, 2064

                def _sc1(v, pg=pg, w=da_s, wy=prev_y_t, wso=s_so.n):
                    v.wait_ge(s_s.h, w)
                    if wy:
                        v.wait_ge(s_t.h, wy)
                    if wso:
                        v.wait_ge(s_so.h, wso)
                    v.tensor_copy(SC[:, C_DB:C_DB + 512], pg[0:16, 1024:1536])
                    v.tensor_tensor(SC[:, C_BXD:C_BXD + 512], pg[0:16, 1536:2048],
                                    SC[:, C_DB:C_DB + 512], ALU.mult)
                    v.tensor_scalar(SC[:, S_SEL1:S_SEL1 + 1], SC[:, S_PREV1:S_PREV1 + 1],
                                    WSEL[0:16, 0:1], None, ALU.mult)
                    v.tensor_tensor_scan(SC[:, C_HS0:C_HS0 + 512], SC[:, C_DA:C_DA + 512],
                                         SC[:, C_BXD:C_BXD + 512], SC[:, S_SEL1:S_SEL1 + 1],
                                         ALU.mult, ALU.add).then_inc(s_v.h, 1)
                PH.v(_sc1)
                s_v.n += 1
                scan1_v = s_v.n

                def _si1(g, wv=scan1_v, wcc=s_cc.n):
                    g.wait_ge(s_v.h, wv)
                    g.wait_ge(s_cc.h, wcc)
                    g.dma_start(out=cs_in,
                                in_=SC[:, C_HS0 + 511:C_HS0 + 512]).then_inc(s_si.h, 16)
                PH.g(_si1)
                s_si.n += 16

                def _agsa(g, wsi=s_si.n, wso=s_so.n):
                    g.wait_ge(s_si.h, wsi)
                    if wso:
                        g.wait_ge(s_so.h, wso)
                    g.collective_compute("AllGather", ALU.bypass, replica_groups=RG,
                                         ins=[cs_in], outs=[cs_out]).then_inc(s_cc.h, 1)
                PH.g(_agsa)
                s_cc.n += 1

                def _so1(g, wcc=s_cc.n, wpv=prev_scan2_v):
                    g.wait_ge(s_cc.h, wcc)
                    if wpv:
                        g.wait_ge(s_v.h, wpv)
                    g.dma_start(out=SC[:, S_CUR0:S_CUR0 + 1], in_=cs_out[0:1, :]).then_inc(s_so.h, 16)
                PH.g(_so1)
                s_so.n += 16
                so1_n = s_so.n

                def _sc2(v, pg=pg, wso=so1_n):
                    v.wait_ge(s_so.h, wso)
                    v.tensor_scalar(SC[:, S_SEL2:S_SEL2 + 1], SC[:, S_CUR0:S_CUR0 + 1],
                                    WSEL[0:16, 1:2], None, ALU.mult)
                    v.tensor_tensor_scan(SC[:, C_HS2:C_HS2 + 512], SC[:, C_DA:C_DA + 512],
                                         SC[:, C_BXD:C_BXD + 512], SC[:, S_SEL2:S_SEL2 + 1],
                                         ALU.mult, ALU.add)
                    v.tensor_scalar(SC[:, C_HS0:C_HS0 + 512], SC[:, C_HS0:C_HS0 + 512],
                                    WSEL[0:16, 0:1], None, ALU.mult)
                    v.scalar_tensor_tensor(SC[:, C_HS0:C_HS0 + 512], SC[:, C_HS2:C_HS2 + 512],
                                           WSEL[0:16, 1:2], SC[:, C_HS0:C_HS0 + 512],
                                           ALU.mult, ALU.add)
                    v.tensor_copy(HSB[:], SC[:, C_HS0:C_HS0 + 512]).then_inc(s_v.h, 1)
                PH.v(_sc2)
                s_v.n += 1
                scan_v = s_v.n
                prev_scan2_v = scan_v
                grp_req[gi] = (s_v, scan1_v)

                if t < nt - 1:
                    def _si2(g, wv=scan_v, wcc=s_cc.n):
                        g.wait_ge(s_v.h, wv)
                        g.wait_ge(s_cc.h, wcc)
                        g.dma_start(out=cs_in, in_=SC[:, C_HS2 + 511:C_HS2 + 512]).then_inc(s_si.h, 16)
                    PH.g(_si2)
                    s_si.n += 16

                    def _agsb(g, wsi=s_si.n, wso=s_so.n):
                        g.wait_ge(s_si.h, wsi)
                        g.wait_ge(s_so.h, wso)
                        g.collective_compute("AllGather", ALU.bypass, replica_groups=RG,
                                             ins=[cs_in], outs=[cs_out]).then_inc(s_cc.h, 1)
                    PH.g(_agsb)
                    s_cc.n += 1

                    def _so2(g, wcc=s_cc.n, wv1=scan1_v):
                        g.wait_ge(s_cc.h, wcc)
                        g.wait_ge(s_v.h, wv1)
                        g.dma_start(out=SC[:, S_PREV1:S_PREV1 + 1], in_=cs_out[1:2, :]).then_inc(s_so.h, 16)
                    PH.g(_so2)
                    s_so.n += 16

            emit_gu(range(4, 12))
            cur_gu_t = s_t.n
            hh_v = s_v.n
            if t + 1 < nt:
                emit_front(t + 1, whh=hh_v)
            for w in range(4):
                gi, pg, req = next_group()

                def _mm(te, req=req, pg=pg, w=w, ws=scan_v):
                    if req is not None:
                        te.wait_ge(req[0].h, req[1])
                    te.wait_ge(s_v.h, ws)
                    i = None
                    for cl in range(4):
                        c = w * 4 + cl
                        te.matmul(pg[:, cl * 512:(cl + 1) * 512], CBF[:, c * 128:(c + 1) * 128],
                                  HSB[:], start=True, stop=False)
                        i = te.matmul(pg[:, cl * 512:(cl + 1) * 512], DBF[:, c * 128:(c + 1) * 128],
                                      XO[:, c * TT:(c + 1) * TT], start=False, stop=True)
                    i.then_inc(s_t.h, 1)
                PH.t(_mm)
                s_t.n += 1
                wv_t = s_t.n

                def _ev(v, pg=pg, w=w, wv=wv_t):
                    v.wait_ge(s_t.h, wv)
                    i = None
                    for cl in range(4):
                        c = w * 4 + cl
                        i = v.tensor_tensor(SPY[:, c * TT:(c + 1) * TT], pg[:, cl * 512:(cl + 1) * 512],
                                            GS[:, c * TT:(c + 1) * TT], ALU.mult)
                    i.then_inc(s_v.h, 1)
                PH.v(_ev)
                s_v.n += 1
                grp_req[gi] = (s_v, s_v.n)
            ybf_v = s_v.n
            prev_y_t_new = s_t.n
            if t + 1 < nt:
                emit_front2(t + 1, cur_gu_t)

            def _outrdy(te, r0=grp_req[0], r1=grp_req[1]):
                if r0 is not None:
                    te.wait_ge(r0[0].h, r0[1])
                if r1 is not None:
                    te.wait_ge(r1[0].h, r1[1])
            PH.t(_outrdy)

            for kg in range(4):
                buf, wn = emit_slab(wov[kg], "wo")

                def _mm(te, buf=buf, wn=wn, kg=kg):
                    te.wait_ge(wn[0].h, wn[1])
                    i = None
                    for kl in range(4):
                        k = kg * 4 + kl
                        for m in range(8):
                            pg = pgA if m < 4 else pgB
                            i = te.matmul(pg[:, (m % 4) * 512:(m % 4) * 512 + 512],
                                          buf[:, kl * 1024 + m * 128:kl * 1024 + m * 128 + 128],
                                          SPY[:, k * TT:(k + 1) * TT],
                                          start=(k == 0), stop=False)
                    i.then_inc(s_t.h, 1)
                PH.t(_mm)
                s_t.n += 1
                slab_done()
            for kg in range(6):
                buf, wn = emit_slab(dnv[kg], "dn")

                def _mm(te, buf=buf, wn=wn, kg=kg, wh=hh_v):
                    te.wait_ge(wn[0].h, wn[1])
                    if kg == 0:
                        te.wait_ge(s_v.h, wh)
                    i = None
                    for kl in range(4):
                        k = kg * 4 + kl
                        for m in range(8):
                            pg = pgA if m < 4 else pgB
                            i = te.matmul(pg[:, (m % 4) * 512:(m % 4) * 512 + 512],
                                          buf[:, kl * 1024 + m * 128:kl * 1024 + m * 128 + 128],
                                          HH[:, k * TT:(k + 1) * TT],
                                          start=False, stop=(k == 23))
                    i.then_inc(s_t.h, 1)
                PH.t(_mm)
                s_t.n += 1
                slab_done()
            out_t = s_t.n

            def _oev(v, w=out_t, t=t):
                v.wait_ge(s_t.h, w)
                if t > 0:
                    v.wait_ge(s_o.h, 16 * t)
                i = None
                for m in range(8):
                    pg = pgA if m < 4 else pgB
                    i = v.tensor_tensor(XOf[:, m * 512:(m + 1) * 512],
                                        pg[:, (m % 4) * 512:(m % 4) * 512 + 512],
                                        XR[:, m * TT:(m + 1) * TT], ALU.add)
                i.then_inc(s_v.h, 1)
            PH.v(_oev)
            s_v.n += 1
            out_v = s_v.n
            grp_req[0] = (s_v, out_v)
            grp_req[1] = (s_v, out_v)
            if t + 1 < nt:
                emit_r(t + 1)

            def _ost(g, t0=t0, w=out_v):
                g.wait_ge(s_v.h, w)
                dst = out_fm.rearrange("(c p) t -> p c t", p=128)[:, :, t0:t0 + TT]
                g.dma_start(out=dst, in_=XOf.rearrange("p (c t) -> p c t", c=DC)).then_inc(s_o.h, 16)
            PH.g(_ost)
            s_o.n += 16

            prev_ybf_v = ybf_v
            prev_win_t = prev_win_t_new
            prev_y_t = prev_y_t_new
            PH.flush()

        if debug:
            @block.gpsimd
            def _(g):
                g.wait_ge(s_o.h, s_o.n)
                for nm, tn in [("d_r", XR), ("d_xn", TB), ("d_hbf", HBF), ("d_xnbf", XNBF),
                               ("d_xspre", XSPRE), ("d_gs", GS), ("d_spy", SPY),
                               ("d_hh", HH), ("d_sc", SC), ("d_sta", STA)]:
                    s_o.inc(g.dma_start(out=dbg[nm], in_=tn[:]), 16)

        @block.gpsimd
        def _(g):
            g.wait_ge(s_o.h, s_o.n)

    return nc
_PROG_CACHE = {}


def _get_prog(nt):
    if nt not in _PROG_CACHE:
        _PROG_CACHE[nt] = build_program(nt, zigzag=True)
    return _PROG_CACHE[nt]


def _win_slabs(w_in):
    a = np.ascontiguousarray(w_in, np.float32).reshape(8, 128, 8, 512)
    return np.ascontiguousarray(a.transpose(2, 1, 0, 3).reshape(8 * 128, 4096))


def _wd_slabs(wd):
    a = np.ascontiguousarray(wd, np.float32).reshape(2, 8, 128, 4, 512)
    return np.ascontiguousarray(a.transpose(3, 0, 2, 1, 4).reshape(8 * 128, 4096))


def _wout_slabs(w_out):
    a = np.ascontiguousarray(w_out, np.float32).reshape(4, 4, 128, 1024)
    return np.ascontiguousarray(a.transpose(0, 2, 1, 3).reshape(4 * 128, 4096))


def _gu_slabs(w_gate, w_up):
    ag = np.ascontiguousarray(np.asarray(w_gate, np.int8).T).reshape(8, 128, 12, 256)
    au = np.ascontiguousarray(np.asarray(w_up, np.int8).T).reshape(8, 128, 12, 256)
    s = np.stack([ag, au], axis=3)
    return np.ascontiguousarray(s.transpose(2, 1, 0, 3, 4).reshape(12 * 128, 4096))


def _dn_slabs(w_down):
    a = np.ascontiguousarray(np.asarray(w_down, np.int8).T).reshape(6, 4, 128, 1024)
    return np.ascontiguousarray(a.transpose(0, 2, 1, 3).reshape(6 * 128, 4096))


def _host_prep(L0_attn_ln_gamma, L0_attn_ln_beta, L0_attn_W_in, L0_attn_conv_w,
               L0_attn_conv_b, L0_attn_A_log, L0_attn_B_mat, L0_attn_C_mat,
               L0_attn_D_vec, L0_attn_Wd, L0_attn_bd, L0_attn_W_out,
               L0_ffn_w_gate, L0_ffn_w_up, L0_ffn_w_down):
    cw = np.asarray(L0_attn_conv_w, np.float32).reshape(DI, 4)
    convd = np.zeros((4 * 128, DIC * 128), np.float32)
    for w in range(4):
        for cl in range(4):
            c = w * 4 + cl
            for k in range(4):
                blk = convd[w * 128:(w + 1) * 128, (cl * 4 + k) * 128:(cl * 4 + k + 1) * 128]
                np.fill_diagonal(blk, cw[c * 128:(c + 1) * 128, k])
    dvec = np.asarray(L0_attn_D_vec, np.float32)
    ddiag = np.zeros((128, DIC * 128), np.float32)
    for c in range(DIC):
        np.fill_diagonal(ddiag[:, c * 128:(c + 1) * 128], dvec[c * 128:(c + 1) * 128])
    return dict(
        win=_win_slabs(L0_attn_W_in),
        wd=_wd_slabs(L0_attn_Wd),
        wout=_wout_slabs(L0_attn_W_out),
        gus=_gu_slabs(L0_ffn_w_gate, L0_ffn_w_up),
        dns=_dn_slabs(L0_ffn_w_down),
        convd=convd, ddiag=ddiag,
        cpre=np.ascontiguousarray(np.asarray(L0_attn_C_mat, np.float32).T),
        bpre=np.ascontiguousarray(np.asarray(L0_attn_B_mat, np.float32).T),
        convb=np.ascontiguousarray(np.asarray(L0_attn_conv_b, np.float32).reshape(DIC, 128).T),
        bdb=np.ascontiguousarray(np.asarray(L0_attn_bd, np.float32).reshape(DIC, 128).T),
        gamv=np.ascontiguousarray(np.asarray(L0_attn_ln_gamma, np.float32).reshape(DC, 128).T),
        betv=np.ascontiguousarray(np.asarray(L0_attn_ln_beta, np.float32).reshape(DC, 128).T),
        alog=np.ascontiguousarray(np.asarray(L0_attn_A_log, np.float32).reshape(1, NS)),
        cvecd=np.repeat(np.array([[EPS, -0.5, -1.0, QS, 1.0 + EPS, EPS * EPS]], np.float32), 128, axis=0),
        wseld=np.zeros((128, 2), np.float32),
    )


def kernel(L0_attn_x, L0_attn_ln_gamma, L0_attn_ln_beta, L0_attn_W_in,
           L0_attn_conv_w, L0_attn_conv_b, L0_attn_A_log, L0_attn_B_mat,
           L0_attn_C_mat, L0_attn_D_vec, L0_attn_Wd, L0_attn_bd,
           L0_attn_W_out, L0_ffn_w_gate, L0_ffn_w_up, L0_ffn_w_down):
    x = np.asarray(L0_attn_x)
    B, L, _ = x.shape
    ntg = L // TT
    nt = ntg // 2

    shared = _host_prep(L0_attn_ln_gamma, L0_attn_ln_beta, L0_attn_W_in,
                        L0_attn_conv_w, L0_attn_conv_b, L0_attn_A_log,
                        L0_attn_B_mat, L0_attn_C_mat, L0_attn_D_vec,
                        L0_attn_Wd, L0_attn_bd, L0_attn_W_out,
                        L0_ffn_w_gate, L0_ffn_w_up, L0_ffn_w_down)

    in_maps = []
    for c in range(8):
        b, par = c // 2, c % 2
        m = dict(shared)
        xt = x[b % B].T.astype(np.float32).reshape(D, ntg, TT)
        m["x_fm"] = np.ascontiguousarray(xt[:, par::2, :].reshape(D, nt * TT))
        ws = np.zeros((128, 2), np.float32)
        ws[:, par] = 1.0
        m["wseld"] = ws
        in_maps.append(m)

    nc = _get_prog(nt)
    res = run_bass_kernel_spmd(nc, in_maps, core_ids=list(range(8)))
    out = np.empty_like(x)
    for b in range(B):
        ot = np.empty((D, ntg, TT), np.float32)
        ot[:, 0::2, :] = res.results[2 * b]["out_fm"].reshape(D, nt, TT)
        ot[:, 1::2, :] = res.results[2 * b + 1]["out_fm"].reshape(D, nt, TT)
        out[b] = ot.reshape(D, L).T
    return out


# version 17
# speedup vs baseline: 1.0054x; 1.0054x over previous
import contextlib
import numpy as np

import concourse.bass as bass
import concourse.mybir as mybir
from concourse.bass_utils import run_bass_kernel_spmd

F32 = mybir.dt.float32
BF16 = mybir.dt.bfloat16
I32 = mybir.dt.int32
I8 = mybir.dt.int8
AF = mybir.ActivationFunctionType
ALU = mybir.AluOpType

D, DC = 1024, 8
DI, DIC = 2048, 16
H, HC = 3072, 24
NS = 16
TT = 512
EPS = 1e-5
QS = 1.0 / 127.0
QS2 = QS * QS

MU, MSQ, M2, R0, R1 = range(5)
VAR0, LA = MSQ, M2
DELTA = MU
C_DA, C_BXD, C_HS0, C_HS1, C_DB = 0, 512, 1024, 1536, 1536


class Sem:
    def __init__(self, stack, nc, name):
        self.h = stack.enter_context(nc.semaphore(name))
        self.n = 0

    def inc(self, inst, k=1):
        inst.then_inc(self.h, k)
        self.n += k


def build_program(nt, zigzag=False, debug=False):
    nc = bass.Bass("TRN2", target_bir_lowering=False, detect_race_conditions=False)
    NT = nt * TT

    x_fm = nc.dram_tensor("x_fm", [D, NT], F32, kind="ExternalInput").ap()
    win = nc.dram_tensor("win", [8 * 128, 4096], F32, kind="ExternalInput").ap()
    wd = nc.dram_tensor("wd", [8 * 128, 4096], F32, kind="ExternalInput").ap()
    wout = nc.dram_tensor("wout", [4 * 128, 4096], F32, kind="ExternalInput").ap()
    gus = nc.dram_tensor("gus", [12 * 128, 4096], I8, kind="ExternalInput").ap()
    dns = nc.dram_tensor("dns", [6 * 128, 4096], I8, kind="ExternalInput").ap()
    convd = nc.dram_tensor("convd", [4 * 128, DIC * 128], F32, kind="ExternalInput").ap()
    ddiag = nc.dram_tensor("ddiag", [128, DIC * 128], F32, kind="ExternalInput").ap()
    cpre = nc.dram_tensor("cpre", [NS, DI], F32, kind="ExternalInput").ap()
    bpre = nc.dram_tensor("bpre", [DI, NS], F32, kind="ExternalInput").ap()
    convb = nc.dram_tensor("convb", [128, DIC], F32, kind="ExternalInput").ap()
    bdb = nc.dram_tensor("bdb", [128, DIC], F32, kind="ExternalInput").ap()
    gamv = nc.dram_tensor("gamv", [128, DC], F32, kind="ExternalInput").ap()
    betv = nc.dram_tensor("betv", [128, DC], F32, kind="ExternalInput").ap()
    alog = nc.dram_tensor("alog", [1, NS], F32, kind="ExternalInput").ap()
    cvecd = nc.dram_tensor("cvecd", [128, 6], F32, kind="ExternalInput").ap()
    wseld = nc.dram_tensor("wseld", [128, 2], F32, kind="ExternalInput").ap()
    ch_in = nc.dram_tensor("ch_in", [1, 6144], BF16, kind="Internal").ap()
    ch_out = nc.dram_tensor("ch_out", [2, 6144], BF16, kind="Internal").ap()
    cs_in = nc.dram_tensor("cs_in", [1, NS], F32, kind="Internal").ap()
    cs_out = nc.dram_tensor("cs_out", [2, NS], F32, kind="Internal").ap()
    RG = [[0, 1], [2, 3], [4, 5], [6, 7]]
    out_fm = nc.dram_tensor("out_fm", [D, NT], F32, kind="ExternalOutput").ap()
    dbg = {}
    if debug:
        for nm, shp, dt in [("d_r", [128, DC * TT], F32), ("d_xn", [128, DC * TT], F32),
                            ("d_hbf", [128, DC * TT], BF16), ("d_xnbf", [128, DC * TT], BF16),
                            ("d_xspre", [128, DIC * 516], BF16),
                            ("d_gs", [128, DIC * TT], F32), ("d_spy", [128, DIC * TT], BF16),
                            ("d_hh", [128, HC * TT], BF16), ("d_sc", [16, 2576], F32),
                            ("d_sta", [1, 5 * TT], F32)]:
            dbg[nm] = nc.dram_tensor(nm, shp, dt, kind="ExternalOutput").ap()

    s_win = nc.dram_tensor("s_win", [8 * 128, 4096], BF16, kind="Internal").ap()
    s_conv = nc.dram_tensor("s_conv", [4 * 128, 2048], BF16, kind="Internal").ap()
    s_wd = nc.dram_tensor("s_wd", [8 * 128, 4096], BF16, kind="Internal").ap()
    s_wo = nc.dram_tensor("s_wo", [4 * 128, 4096], BF16, kind="Internal").ap()
    s_gu = nc.dram_tensor("s_gu", [12 * 128, 4096], BF16, kind="Internal").ap()
    s_dn = nc.dram_tensor("s_dn", [6 * 128, 4096], BF16, kind="Internal").ap()

    stck = contextlib.ExitStack()
    with stck:
        XR = stck.enter_context(nc.sbuf_tensor("XR", [128, DC * TT], F32))
        TB = stck.enter_context(nc.sbuf_tensor("TB", [128, DC * TT], F32))
        XNBF = stck.enter_context(nc.sbuf_tensor("XNBF", [128, DC * TT], BF16))
        HBF = stck.enter_context(nc.sbuf_tensor("HBF", [128, DC * TT], BF16))
        XSPRE = stck.enter_context(nc.sbuf_tensor("XSPRE", [128, DIC * 516], BF16))
        HH = stck.enter_context(nc.sbuf_tensor("HH", [128, HC * TT], BF16))
        GS = stck.enter_context(nc.sbuf_tensor("GS", [128, DIC * TT], F32))
        SPY = stck.enter_context(nc.sbuf_tensor("SPY", [128, DIC * TT], BF16))
        XO = stck.enter_context(nc.sbuf_tensor("XO", [128, DIC * TT], BF16))
        WP = [stck.enter_context(nc.sbuf_tensor(f"WP{i}", [128, 4096], BF16)) for i in range(3)]
        SC = stck.enter_context(nc.sbuf_tensor("SC", [16, 2576], F32))
        HPK = stck.enter_context(nc.sbuf_tensor("HPK", [128, 48], BF16))
        HIN = stck.enter_context(nc.sbuf_tensor("HIN", [128, 96], BF16))
        HPR = stck.enter_context(nc.sbuf_tensor("HPR", [128, 48], BF16))
        HTMP = stck.enter_context(nc.sbuf_tensor("HTMP", [128, 48], BF16))
        WSEL = stck.enter_context(nc.sbuf_tensor("WSEL", [128, 2], F32))
        HSB = stck.enter_context(nc.sbuf_tensor("HSB", [16, 512], BF16))
        STA = stck.enter_context(nc.sbuf_tensor("STA", [1, 5 * TT], F32))
        GAM = stck.enter_context(nc.sbuf_tensor("GAM", [128, DC], F32))
        BET = stck.enter_context(nc.sbuf_tensor("BET", [128, DC], F32))
        CVB = stck.enter_context(nc.sbuf_tensor("CVB", [128, DIC], F32))
        BDBs = stck.enter_context(nc.sbuf_tensor("BDBs", [128, DIC], F32))
        CBF = stck.enter_context(nc.sbuf_tensor("CBF", [16, DI], BF16))
        DBF = stck.enter_context(nc.sbuf_tensor("DBF", [128, DIC * 128], BF16))
        BBF = stck.enter_context(nc.sbuf_tensor("BBF", [128, DIC * NS], BF16))
        AROW = stck.enter_context(nc.sbuf_tensor("AROW", [1, NS], F32))
        ONE1 = stck.enter_context(nc.sbuf_tensor("ONE1", [1, 128], F32))
        ONEB = stck.enter_context(nc.sbuf_tensor("ONEB", [128, 1], BF16))
        ONEF = stck.enter_context(nc.sbuf_tensor("ONEF", [128, 1], F32))
        ON16 = stck.enter_context(nc.sbuf_tensor("ON16", [1, NS], F32))
        CVEC = stck.enter_context(nc.sbuf_tensor("CVEC", [128, 6], F32))
        pgA = stck.enter_context(nc.psum_tensor("pgA", [128, 2048], F32))
        pgB = stck.enter_context(nc.psum_tensor("pgB", [128, 2048], F32))

        s_x = Sem(stck, nc, "s_x")
        s_w = [Sem(stck, nc, "s_w0"), Sem(stck, nc, "s_w1"), Sem(stck, nc, "s_w2")]
        s_t = Sem(stck, nc, "s_t")
        s_v = Sem(stck, nc, "s_v")
        s_s = Sem(stck, nc, "s_s")
        s_o = Sem(stck, nc, "s_o")
        s_cl = [Sem(stck, nc, "s_cl0"), Sem(stck, nc, "s_cl1")]
        s_cs = [Sem(stck, nc, "s_cs0"), Sem(stck, nc, "s_cs1")]
        s_cc = Sem(stck, nc, "s_cc")
        s_k = Sem(stck, nc, "s_k")
        s_hi = Sem(stck, nc, "s_hi")
        s_ho = Sem(stck, nc, "s_ho")
        s_si = Sem(stck, nc, "s_si")
        s_so = Sem(stck, nc, "s_so")
        block = stck.enter_context(nc.Block())

        XOf = XO.ap().bitcast(F32)
        XSPf = XSPRE.ap().bitcast(F32)

        def sta(i):
            return STA[0:1, i * TT:(i + 1) * TT]

        def xsp(c, lo, hi):
            return XSPRE[:, c * 516 + lo:c * 516 + hi]

        class PhaseBuf:
            def __init__(self):
                self.q = {"t": [], "v": [], "s": [], "g": [], "y": []}

            def t(self, f):
                self.q["t"].append(f)

            def v(self, f):
                self.q["v"].append(f)

            def s(self, f):
                self.q["s"].append(f)

            def g(self, f):
                self.q["g"].append(f)

            def y(self, f):
                self.q["y"].append(f)

            def flush(self):
                qs = self.q
                self.q = {"t": [], "v": [], "s": [], "g": [], "y": []}
                if qs["y"]:
                    @block.sync
                    def _(e, fs=qs["y"]):
                        for f in fs:
                            f(e)
                if qs["g"]:
                    @block.gpsimd
                    def _(e, fs=qs["g"]):
                        for f in fs:
                            f(e)
                if qs["t"]:
                    @block.tensor
                    def _(e, fs=qs["t"]):
                        for f in fs:
                            f(e)
                if qs["v"]:
                    @block.vector
                    def _(e, fs=qs["v"]):
                        for f in fs:
                            f(e)
                if qs["s"]:
                    @block.scalar
                    def _(e, fs=qs["s"]):
                        for f in fs:
                            f(e)

        PH = PhaseBuf()

        jobs = []
        winv = s_win.rearrange("(s p) c -> s p c", p=128)
        srcwin = win.rearrange("(s p) c -> s p c", p=128)
        for s in range(8):
            jobs.append((srcwin[s], F32, winv[s], 4096))
        cvv = s_conv.rearrange("(s p) c -> s p c", p=128)
        for w in range(4):
            jobs.append((convd[w * 128:(w + 1) * 128, :], F32, cvv[w], 2048))
        wdv = s_wd.rearrange("(s p) c -> s p c", p=128)
        srcwd = wd.rearrange("(s p) c -> s p c", p=128)
        for s in range(8):
            jobs.append((srcwd[s], F32, wdv[s], 4096))
        guv = s_gu.rearrange("(s p) c -> s p c", p=128)
        srcgu = gus.rearrange("(s p) c -> s p c", p=128)
        for w in range(12):
            jobs.append((srcgu[w], I8, guv[w], 4096))
        wov = s_wo.rearrange("(s p) c -> s p c", p=128)
        srcwo = wout.rearrange("(s p) c -> s p c", p=128)
        for s in range(4):
            jobs.append((srcwo[s], F32, wov[s], 4096))
        dnv = s_dn.rearrange("(s p) c -> s p c", p=128)
        srcdn = dns.rearrange("(s p) c -> s p c", p=128)
        for s in range(6):
            jobs.append((srcdn[s], I8, dnv[s], 4096))

        XOc = XO.ap().bitcast(F32)
        SPYc = SPY.ap().bitcast(F32)

        def _consts(s):
            for dst, src in [(GAM, gamv), (BET, betv), (CVB, convb), (BDBs, bdb)]:
                s.dma_start(out=dst[:], in_=src).then_inc(s_k.h, 16)
            s.dma_start(out=AROW[:], in_=alog).then_inc(s_k.h, 16)
            s.dma_start(out=CVEC[:], in_=cvecd).then_inc(s_k.h, 16)
            s.dma_start(out=WSEL[:], in_=wseld).then_inc(s_k.h, 16)
            s.dma_start(out=XOc[0:16, 0:DI], in_=cpre).then_inc(s_k.h, 16)
            s.dma_start(
                out=SPYc[:, 0:256].rearrange("p (a b) -> p a b", a=DIC),
                in_=bpre.rearrange("(c p) n -> p c n", p=128)).then_inc(s_k.h, 16)
            s.dma_start(out=XOc[:, DI:2 * DI], in_=ddiag).then_inc(s_k.h, 16)
        PH.s(_consts)
        s_k.n += 16 * 10
        consts_lc = s_k.n

        def _arow(s):
            s.wait_ge(s_k.h, consts_lc)
            s.activation(AROW[:], AROW[:], AF.Exp).then_inc(s_s.h, 1)
        PH.s(_arow)
        s_s.n += 1
        arow_sc = s_s.n

        def _cinit(v):
            v.memset(ONE1[:], 1.0)
            v.memset(ONEB[:], 1.0)
            v.memset(ONEF[:], 1.0)
            v.memset(ON16[:], 1.0)
            v.memset(XSPRE[:], 0.0)
            v.memset(SC[:], 0.0)
            v.memset(HPR[:], 0.0)
            v.memset(HIN[:], 0.0)
            v.wait_ge(s_k.h, consts_lc)
            v.wait_ge(s_s.h, arow_sc)
            v.tensor_scalar(AROW[:], AROW[:], -1.0, None, ALU.mult)
            v.tensor_copy(CBF[:], XOc[0:16, 0:DI])
            v.tensor_copy(BBF[:], SPYc[:, 0:256])
            v.tensor_copy(DBF[:], XOc[:, DI:2 * DI]).then_inc(s_v.h, 1)
        PH.v(_cinit)
        s_v.n += 1
        consts_done_v = s_v.n
        cast_counts = []
        store_counts = []
        grp_ready = {}
        _grp_end = {7: "win", 11: "conv", 19: "wd", 31: "gu", 35: "wo", 41: "dn"}
        GSi8 = GS.ap().bitcast(I8)
        for j, (src, dt, dst, wj) in enumerate(jobs):
            par = j % 2
            if dt == I8:
                fstage = GSi8[:, par * 16384:par * 16384 + wj]
            else:
                fstage = GS[:, par * 4096:par * 4096 + wj]
            bstage = HH[:, par * 4096:par * 4096 + wj]
            scl = s_cl[par]
            scs = s_cs[par]

            def _ld(s, src=src, dst_=fstage, j=j, scl=scl):
                if j >= 2:
                    s.wait_ge(s_v.h, cast_counts[j - 2])
                s.dma_start(out=dst_, in_=src).then_inc(scl.h, 16)
            PH.s(_ld)
            scl.n += 16
            lc = scl.n

            def _cast(v, fs=fstage, bs=bstage, j=j, lc=lc, scl=scl, scs=scs):
                v.wait_ge(scl.h, lc)
                if j >= 2:
                    v.wait_ge(scs.h, store_counts[j - 2])
                v.tensor_copy(bs, fs).then_inc(s_v.h, 1)
            PH.v(_cast)
            s_v.n += 1
            cast_counts.append(s_v.n)

            def _st(g, dst=dst, wj=wj, bs=bstage, vc=s_v.n, scs=scs):
                g.wait_ge(s_v.h, vc)
                g.dma_start(out=dst[:, 0:wj], in_=bs[:, 0:wj]).then_inc(scs.h, 16)
            PH.g(_st)
            scs.n += 16
            store_counts.append(scs.n)
            if j in _grp_end:
                grp_ready[_grp_end[j]] = (s_cs[0].n, s_cs[1].n)

        PH.flush()

        grp = [pgA, pgB]
        grp_req = [None, None]
        wave_ctr = [0]

        def next_group():
            gi = wave_ctr[0] % 2
            wave_ctr[0] += 1
            return gi, grp[gi], grp_req[gi]

        slab_ctr = [0]
        slab_free = []

        seen_grp = set()

        def emit_slab(src2, grp, width=4096):
            i = slab_ctr[0]
            slab_ctr[0] += 1
            par = i % 3
            buf = WP[par]
            sw = s_w[par]
            gw = None
            if grp not in seen_grp:
                seen_grp.add(grp)
                gw = grp_ready.get(grp)
            free_at = slab_free[i - 3] if i >= 3 else None

            def _dma(sy, src2=src2, buf=buf, width=width, sw=sw, gw=gw, free_at=free_at):
                if gw is not None:
                    sy.wait_ge(s_cs[0].h, gw[0])
                    sy.wait_ge(s_cs[1].h, gw[1])
                if free_at is not None:
                    sy.wait_ge(s_t.h, free_at)
                sy.dma_start(out=buf[:, 0:width], in_=src2).then_inc(sw.h, 16)
            PH.y(_dma)
            sw.n += 16
            return buf, (sw, sw.n)

        def slab_done():
            slab_free.append(s_t.n)

        FR = {}

        def emit_front(ft, first=False, whh=0):
            t0f = ft * TT

            def _xld(g, t0f=t0f, wcv=prev_conv_t, whp=halo_v if not zigzag else hpk_v_last[0],
                     wc0=(consts_done_v if first else 0)):
                if wc0:
                    g.wait_ge(s_v.h, wc0)
                if wcv:
                    g.wait_ge(s_t.h, wcv)
                if whp:
                    g.wait_ge(s_v.h, whp)
                srcx = x_fm.rearrange("(c p) t -> p c t", p=128)[:, :, t0f:t0f + TT]
                g.dma_start(out=XSPf[:, 0:4096].rearrange("p (c t) -> p c t", c=DC),
                            in_=srcx).then_inc(s_x.h, 16)
            PH.g(_xld)
            s_x.n += 16
            x_ld = s_x.n

            def _xsq(s, x_ld=x_ld, w8=(whh or consts_done_v)):
                s.wait_ge(s_x.h, x_ld)
                s.wait_ge(s_v.h, w8)
                s.activation(TB[:], XSPf[:, 0:4096], AF.Square).then_inc(s_s.h, 1)
            PH.s(_xsq)
            s_s.n += 1
            xsq_s = s_s.n

            gi1, pg1, req1 = next_group()

            def _stats(te, x_ld=x_ld, xsq_s=xsq_s, req=req1, pg=pg1):
                te.wait_ge(s_x.h, x_ld)
                if req is not None:
                    te.wait_ge(req[0].h, req[1])
                for c in range(DC):
                    te.matmul(pg[0:1, 0:512], ONEF[:], XSPf[:, c * TT:(c + 1) * TT],
                              start=(c == 0), stop=(c == DC - 1))
                te.wait_ge(s_s.h, xsq_s)
                i = None
                for c in range(DC):
                    i = te.matmul(pg[0:1, 512:1024], ONEF[:], TB[:, c * TT:(c + 1) * TT],
                                  start=(c == 0), stop=(c == DC - 1))
                i.then_inc(s_t.h, 1)
            PH.t(_stats)
            s_t.n += 1
            stats_t = s_t.n

            def _tv1(v, w=stats_t, pg=pg1):
                v.wait_ge(s_t.h, w)
                v.tensor_scalar(sta(MU), pg[0:1, 0:512], 1.0 / D, None, ALU.mult)
                v.tensor_scalar(sta(MSQ), pg[0:1, 512:1024], 1.0 / D, None, ALU.mult)
                v.tensor_tensor(sta(M2), sta(MU), sta(MU), ALU.mult)
                v.tensor_tensor(sta(VAR0), sta(MSQ), sta(M2), ALU.subtract).then_inc(s_v.h, 1)
            PH.v(_tv1)
            s_v.n += 1
            st_v = s_v.n
            grp_req[gi1] = (s_v, st_v)

            def _ts1(s, w=st_v):
                s.wait_ge(s_v.h, w)
                s.activation(sta(LA), sta(VAR0), AF.Ln, bias=CVEC[0:1, 0:1])
                s.activation(sta(R0), sta(LA), AF.Exp, scale=CVEC[0:1, 1:2])
                s.activation(sta(LA), sta(VAR0), AF.Ln, bias=CVEC[0:1, 5:6], scale=CVEC[0:1, 4:5])
                s.activation(sta(R1), sta(LA), AF.Exp, scale=CVEC[0:1, 1:2]).then_inc(s_s.h, 1)
            PH.s(_ts1)
            s_s.n += 1
            st_s = s_s.n
            FR[ft] = dict(x_ld=x_ld, st_s=st_s)

        def emit_front2(ft, cur_gu_t):
            st_s = FR[ft]["st_s"]
            gi2, pg2, req2 = next_group()

            def _bc(te, w=st_s, req=req2, pg=pg2):
                te.wait_ge(s_s.h, w)
                if req is not None:
                    te.wait_ge(req[0].h, req[1])
                te.matmul(pg[:, 0:512], ONE1[:], sta(MU), start=True, stop=True)
                te.matmul(pg[:, 512:1024], ONE1[:], sta(R0), start=True, stop=True)
                te.matmul(pg[:, 1024:1536], ONE1[:], sta(R1), start=True, stop=True).then_inc(s_t.h, 1)
            PH.t(_bc)
            s_t.n += 1
            bc_t = s_t.n

            def _ln(v, w=bc_t, wgu=cur_gu_t, pg=pg2):
                v.wait_ge(s_t.h, w)
                v.wait_ge(s_t.h, wgu)
                for c in range(DC):
                    v.tensor_tensor(TB[:, c * TT:(c + 1) * TT], XSPf[:, c * TT:(c + 1) * TT],
                                    pg[:, 0:512], ALU.subtract)
                for c in range(DC):
                    v.tensor_tensor(GS[:, 4096 + c * 512:4096 + (c + 1) * 512],
                                    TB[:, c * TT:(c + 1) * TT], pg[:, 1024:1536], ALU.mult)
                for c in range(DC):
                    v.tensor_tensor(TB[:, c * TT:(c + 1) * TT], TB[:, c * TT:(c + 1) * TT],
                                    pg[:, 512:1024], ALU.mult)
                v.tensor_copy(XNBF[:], TB[:]).then_inc(s_v.h, 1)
            PH.v(_ln)
            s_v.n += 1
            ln_v = s_v.n
            grp_req[gi2] = (s_v, ln_v)

            def _hbf(s, w=ln_v, wwin=prev_win_t):
                s.wait_ge(s_v.h, w)
                if wwin:
                    s.wait_ge(s_t.h, wwin)
                i = None
                for c in range(DC):
                    i = s.activation(HBF[:, c * TT:(c + 1) * TT], GS[:, 4096 + c * 512:4096 + (c + 1) * 512],
                                     AF.Identity, bias=BET[:, c:c + 1], scale=GAM[:, c:c + 1])
                i.then_inc(s_s.h, 1)
            PH.s(_hbf)
            s_s.n += 1
            FR[ft]["ln_v"] = ln_v
            FR[ft]["hbf_s"] = s_s.n

        def emit_r(ft):
            def _r(v):
                v.tensor_tensor(XR[:], XSPf[:, 0:4096], TB[:], ALU.add).then_inc(s_v.h, 1)
            PH.v(_r)
            s_v.n += 1
            FR[ft]["r_v"] = s_v.n
            if not zigzag:
                def _hrest(v, z=(ft == 0)):
                    if z:
                        v.memset(HPK[:], 0.0)
                    v.tensor_copy(
                        XSPRE.ap().rearrange("p (c t) -> p c t", c=DIC)[:, :, 0:3],
                        HPK.ap().rearrange("p (c t) -> p c t", c=DIC)).then_inc(s_v.h, 1)
                PH.v(_hrest)
                s_v.n += 1
            if not zigzag:
                FR[ft]["halo_v"] = s_v.n

        hpk_v_last = [0]
        halo_v = 0
        halo_used_v = 0
        prev_ybf_v = 0
        prev_scan2_v = 0
        prev_hpr_v = 0
        prev_conv_t = 0
        prev_win_t = 0
        prev_y_t = 0

        for t in range(nt):
            t0 = t * TT

            if t == 0:
                emit_front(0, first=True)
                emit_front2(0, 0)
                emit_r(0)
            fr = FR[t]
            x_ld = fr["x_ld"]
            ln_v = fr["ln_v"]
            hbf_s = fr["hbf_s"]
            r_v = fr["r_v"]

            win_evac = []
            for mg in range(8):
                buf, wn = emit_slab(winv[mg], "win")
                gi, pg, req = next_group()

                def _mm(te, buf=buf, wn=wn, req=req, pg=pg, w=hbf_s):
                    te.wait_ge(wn[0].h, wn[1])
                    if req is not None:
                        te.wait_ge(req[0].h, req[1])
                    te.wait_ge(s_s.h, w)
                    i = None
                    for ml in range(4):
                        for k in range(8):
                            i = te.matmul(pg[:, ml * 512:(ml + 1) * 512],
                                          buf[:, k * 512 + ml * 128:k * 512 + ml * 128 + 128],
                                          HBF[:, k * TT:(k + 1) * TT],
                                          start=(k == 0), stop=(k == 7))
                    i.then_inc(s_t.h, 1)
                PH.t(_mm)
                s_t.n += 1
                slab_done()
                wv_t = s_t.n

                def _ev(s, pg=pg, mg=mg, w=wv_t, whal=halo_v, wybf=prev_ybf_v, wr=r_v):
                    s.wait_ge(s_t.h, w)
                    i = None
                    if mg < 4:
                        s.wait_ge(s_v.h, wr)
                        if whal:
                            s.wait_ge(s_v.h, whal)
                        for ml in range(4):
                            c = mg * 4 + ml
                            i = s.copy(xsp(c, 3, 515), pg[:, ml * 512:(ml + 1) * 512])
                    else:
                        if mg == 4 and wybf:
                            s.wait_ge(s_v.h, wybf)
                        for ml in range(4):
                            c = (mg - 4) * 4 + ml
                            i = s.activation(GS[:, c * TT:(c + 1) * TT],
                                             pg[:, ml * 512:(ml + 1) * 512], AF.Silu)
                    i.then_inc(s_s.h, 1)
                PH.s(_ev)
                s_s.n += 1
                grp_req[gi] = (s_s, s_s.n)
                win_evac.append(s_s.n)
            prev_win_t_new = s_t.n

            if zigzag:
                def _hpk(v, w=win_evac[3], wcv=prev_conv_t):
                    v.wait_ge(s_s.h, w)
                    if wcv:
                        v.wait_ge(s_t.h, wcv)
                    v.tensor_copy(HPK.ap().rearrange("p (c t) -> p c t", c=DIC),
                                  XSPRE.ap().rearrange("p (c t) -> p c t", c=DIC)[:, :, 512:515]).then_inc(s_v.h, 1)
                PH.v(_hpk)
                s_v.n += 1
                hpk_v = s_v.n
                hpk_v_last[0] = hpk_v

                def _hin(g, wv=hpk_v, wcc=s_cc.n, who=s_ho.n):
                    g.wait_ge(s_v.h, wv)
                    if wcc:
                        g.wait_ge(s_cc.h, wcc)
                    g.dma_start(out=ch_in.rearrange("a (p c) -> a p c", p=128)[0], in_=HPK[:]).then_inc(s_hi.h, 16)
                PH.g(_hin)
                s_hi.n += 16

                def _agh(g, whi=s_hi.n, who=s_ho.n):
                    g.wait_ge(s_hi.h, whi)
                    if who:
                        g.wait_ge(s_ho.h, who)
                    g.collective_compute("AllGather", ALU.bypass, replica_groups=RG,
                                         ins=[ch_in], outs=[ch_out]).then_inc(s_cc.h, 1)
                PH.g(_agh)
                s_cc.n += 1

                def _hout(g, wcc=s_cc.n, wprv=prev_hpr_v):
                    g.wait_ge(s_cc.h, wcc)
                    if wprv:
                        g.wait_ge(s_v.h, wprv)
                    g.dma_start(out=HIN[:, 0:48], in_=ch_out.rearrange("a (p c) -> a p c", p=128)[0]).then_inc(s_ho.h, 16)
                    g.dma_start(out=HIN[:, 48:96], in_=ch_out.rearrange("a (p c) -> a p c", p=128)[1]).then_inc(s_ho.h, 16)
                PH.g(_hout)
                s_ho.n += 32

                def _husev(v, who=s_ho.n, wcv=prev_conv_t, wr=r_v):
                    v.wait_ge(s_ho.h, who)
                    v.wait_ge(s_v.h, wr)
                    if wcv:
                        v.wait_ge(s_t.h, wcv)
                    v.tensor_scalar(HTMP[:], HPR[:], WSEL[:, 0:1], None, ALU.mult)
                    v.scalar_tensor_tensor(
                        XSPRE.ap().rearrange("p (c t) -> p c t", c=DIC)[:, :, 0:3],
                        HIN[:, 0:48].rearrange("p (c t) -> p c t", c=DIC),
                        WSEL[:, 1:2], HTMP.ap().rearrange("p (c t) -> p c t", c=DIC),
                        ALU.mult, ALU.add)
                    v.tensor_copy(HPR[:], HIN[:, 48:96]).then_inc(s_v.h, 1)
                PH.v(_husev)
                s_v.n += 1
                halo_used_v = s_v.n
                prev_hpr_v = halo_used_v

            for w in range(4):
                buf, wn = emit_slab(cvv[w], "conv", width=2048)
                gi, pg, req = next_group()

                def _mm(te, buf=buf, wn=wn, req=req, pg=pg, w=w, wev=win_evac[w],
                        whv=(halo_used_v if zigzag else FR[t].get("halo_v", 0))):
                    te.wait_ge(wn[0].h, wn[1])
                    if req is not None:
                        te.wait_ge(req[0].h, req[1])
                    te.wait_ge(s_s.h, wev)
                    if whv:
                        te.wait_ge(s_v.h, whv)
                    i = None
                    for cl in range(4):
                        c = w * 4 + cl
                        for k in range(4):
                            i = te.matmul(pg[:, cl * 512:(cl + 1) * 512],
                                          buf[:, (cl * 4 + k) * 128:(cl * 4 + k + 1) * 128],
                                          xsp(c, k, k + 512),
                                          start=(k == 0), stop=(k == 3))
                    i.then_inc(s_t.h, 1)
                PH.t(_mm)
                s_t.n += 1
                slab_done()
                wv_t = s_t.n

                def _ev(s, pg=pg, w=w, wv=wv_t, wo=16 * t):
                    s.wait_ge(s_t.h, wv)
                    if w == 0 and wo:
                        s.wait_ge(s_o.h, wo)
                    i = None
                    for cl in range(4):
                        c = w * 4 + cl
                        i = s.activation(XO[:, c * TT:(c + 1) * TT], pg[:, cl * 512:(cl + 1) * 512],
                                         AF.Silu, bias=CVB[:, c:c + 1])
                    i.then_inc(s_s.h, 1)
                PH.s(_ev)
                s_s.n += 1
                grp_req[gi] = (s_s, s_s.n)
            conv_s = s_s.n
            conv_t = s_t.n

            prev_conv_t = conv_t
            if not zigzag:
                def _halo(v, w=conv_t):
                    v.wait_ge(s_t.h, w)
                    v.tensor_copy(
                        HPK.ap().rearrange("p (c t) -> p c t", c=DIC),
                        XSPRE.ap().rearrange("p (c t) -> p c t", c=DIC)[:, :, 512:515]).then_inc(s_v.h, 1)
                PH.v(_halo)
                s_v.n += 1
                hpk_v_last[0] = s_v.n

            for mg in range(4):
                gi, pg, req = next_group()
                for kh in range(2):
                    buf, wn = emit_slab(wdv[mg * 2 + kh], "wd")

                    def _mm(te, buf=buf, wn=wn, req=req, pg=pg, kh=kh, wc=conv_s):
                        te.wait_ge(wn[0].h, wn[1])
                        if kh == 0:
                            if req is not None:
                                te.wait_ge(req[0].h, req[1])
                            te.wait_ge(s_s.h, wc)
                        i = None
                        for ml in range(4):
                            for kl in range(8):
                                k = kh * 8 + kl
                                i = te.matmul(pg[:, ml * 512:(ml + 1) * 512],
                                              buf[:, kl * 512 + ml * 128:kl * 512 + ml * 128 + 128],
                                              XO[:, k * TT:(k + 1) * TT],
                                              start=(k == 0), stop=(k == 15))
                        i.then_inc(s_t.h, 1)
                    PH.t(_mm)
                    s_t.n += 1
                    slab_done()
                wv_t = s_t.n

                def _ev(s, pg=pg, mg=mg, w=wv_t):
                    s.wait_ge(s_t.h, w)
                    for ml in range(4):
                        c = mg * 4 + ml
                        s.activation(pg[:, ml * 512:(ml + 1) * 512], pg[:, ml * 512:(ml + 1) * 512],
                                     AF.Exp, bias=BDBs[:, c:c + 1])
                    i = None
                    for ml in range(4):
                        c = mg * 4 + ml
                        i = s.activation(SPY[:, c * TT:(c + 1) * TT], pg[:, ml * 512:(ml + 1) * 512],
                                         AF.Ln, bias=1.0)
                    i.then_inc(s_s.h, 1)
                PH.s(_ev)
                s_s.n += 1
                grp_req[gi] = (s_s, s_s.n)
            sp_s = s_s.n

            gu_stt = []

            def emit_gu(ws):
                for w in ws:
                    buf, wn = emit_slab(guv[w], "gu")
                    gi, pg, req = next_group()

                    def _mm(te, buf=buf, wn=wn, req=req, w=w, pg=pg, wln=ln_v):
                        te.wait_ge(wn[0].h, wn[1])
                        if req is not None:
                            te.wait_ge(req[0].h, req[1])
                        if w == 0:
                            te.wait_ge(s_v.h, wln)
                        i = None
                        for gl in range(4):
                            for k in range(8):
                                i = te.matmul(pg[:, gl * 512:(gl + 1) * 512],
                                              buf[:, k * 512 + gl * 128:k * 512 + gl * 128 + 128],
                                              XNBF[:, k * TT:(k + 1) * TT],
                                              start=(k == 0), stop=(k == 7))
                        i.then_inc(s_t.h, 1)
                    PH.t(_mm)
                    s_t.n += 1
                    slab_done()
                    wv_t = s_t.n
                    reg = (w % 4) * 1024

                    def _ev(s, pg=pg, wv=wv_t, w=w, reg=reg):
                        s.wait_ge(s_t.h, wv)
                        if w >= 4:
                            s.wait_ge(s_v.h, gu_stt[w - 4])
                        s.activation(TB[:, reg:reg + 512], pg[:, 0:512], AF.Silu, scale=CVEC[:, 3:4])
                        s.activation(TB[:, reg + 512:reg + 1024], pg[:, 512:1024], AF.Silu, scale=CVEC[:, 3:4]).then_inc(s_s.h, 1)
                    PH.s(_ev)
                    s_s.n += 1
                    gsil_s = s_s.n

                    def _hh(v, pg=pg, w=w, wg=gsil_s, reg=reg):
                        v.wait_ge(s_s.h, wg)
                        v.scalar_tensor_tensor(HH[:, (2 * w) * TT:(2 * w + 1) * TT], pg[:, 1024:1536],
                                               QS2, TB[:, reg:reg + 512], ALU.mult, ALU.mult)
                        i = v.scalar_tensor_tensor(HH[:, (2 * w + 1) * TT:(2 * w + 2) * TT], pg[:, 1536:2048],
                                                   QS2, TB[:, reg + 512:reg + 1024], ALU.mult, ALU.mult)
                        i.then_inc(s_v.h, 1)
                    PH.v(_hh)
                    s_v.n += 1
                    gu_stt.append(s_v.n)
                    grp_req[gi] = (s_v, s_v.n)

            emit_gu(range(4))
            gi, pg, req = next_group()

            def _dl(te, req=req, pg=pg, w=sp_s):
                if req is not None:
                    te.wait_ge(req[0].h, req[1])
                te.wait_ge(s_s.h, w)
                i = None
                for c in range(DIC):
                    i = te.matmul(pg[0:1, 0:512], ONEB[:], SPY[:, c * TT:(c + 1) * TT],
                                  start=(c == 0), stop=(c == DIC - 1))
                i.then_inc(s_t.h, 1)
            PH.t(_dl)
            s_t.n += 1
            dl_t = s_t.n

            def _dlv(v, pg=pg, w=dl_t):
                v.wait_ge(s_t.h, w)
                v.tensor_scalar(sta(DELTA), pg[0:1, 0:512], 1.0 / DI, None, ALU.mult).then_inc(s_v.h, 1)
            PH.v(_dlv)
            s_v.n += 1
            dl_v = s_v.n

            def _ssm(te, pg=pg, w=dl_v):
                te.wait_ge(s_v.h, w)
                te.matmul(pg[0:16, 512:1024], AROW[:], sta(DELTA), start=True, stop=True)
                te.matmul(pg[0:16, 1024:1536], ON16[:], sta(DELTA), start=True, stop=True)
                i = None
                for c in range(DIC):
                    i = te.matmul(pg[0:16, 1536:2048], BBF[:, c * NS:(c + 1) * NS],
                                  XO[:, c * TT:(c + 1) * TT], start=(c == 0), stop=(c == DIC - 1))
                i.then_inc(s_t.h, 1)
            PH.t(_ssm)
            s_t.n += 1
            ssm_t = s_t.n

            def _da(s, pg=pg, w=ssm_t):
                s.wait_ge(s_t.h, w)
                s.activation(SC[:, C_DA:C_DA + 512], pg[0:16, 512:1024], AF.Exp).then_inc(s_s.h, 1)
            PH.s(_da)
            s_s.n += 1
            da_s = s_s.n

            if not zigzag:
                hs_cur = C_HS0 if t % 2 == 0 else C_HS1
                hs_prev = C_HS1 if t % 2 == 0 else C_HS0

                def _scan(v, pg=pg, w=da_s, hc=hs_cur, hp=hs_prev, wy=prev_y_t, t=t):
                    v.wait_ge(s_s.h, w)
                    if wy:
                        v.wait_ge(s_t.h, wy)
                    v.tensor_copy(SC[:, C_DB:C_DB + 512], pg[0:16, 1024:1536])
                    v.tensor_tensor(SC[:, C_BXD:C_BXD + 512], pg[0:16, 1536:2048],
                                    SC[:, C_DB:C_DB + 512], ALU.mult)
                    init = 0.0 if t == 0 else SC[:, hp + 511:hp + 512]
                    v.tensor_tensor_scan(SC[:, hc:hc + 512], SC[:, C_DA:C_DA + 512],
                                         SC[:, C_BXD:C_BXD + 512], init, ALU.mult, ALU.add)
                    v.tensor_copy(HSB[:], SC[:, hc:hc + 512]).then_inc(s_v.h, 1)
                PH.v(_scan)
                s_v.n += 1
                scan_v = s_v.n
                grp_req[gi] = (s_v, scan_v)
            else:
                S_SEL1, S_SEL2, S_PREV1, S_CUR0, C_HS2 = 2048, 2049, 2050, 2051, 2064

                def _sc1(v, pg=pg, w=da_s, wy=prev_y_t, wso=s_so.n):
                    v.wait_ge(s_s.h, w)
                    if wy:
                        v.wait_ge(s_t.h, wy)
                    if wso:
                        v.wait_ge(s_so.h, wso)
                    v.tensor_copy(SC[:, C_DB:C_DB + 512], pg[0:16, 1024:1536])
                    v.tensor_tensor(SC[:, C_BXD:C_BXD + 512], pg[0:16, 1536:2048],
                                    SC[:, C_DB:C_DB + 512], ALU.mult)
                    v.tensor_scalar(SC[:, S_SEL1:S_SEL1 + 1], SC[:, S_PREV1:S_PREV1 + 1],
                                    WSEL[0:16, 0:1], None, ALU.mult)
                    v.tensor_tensor_scan(SC[:, C_HS0:C_HS0 + 512], SC[:, C_DA:C_DA + 512],
                                         SC[:, C_BXD:C_BXD + 512], SC[:, S_SEL1:S_SEL1 + 1],
                                         ALU.mult, ALU.add).then_inc(s_v.h, 1)
                PH.v(_sc1)
                s_v.n += 1
                scan1_v = s_v.n

                def _si1(g, wv=scan1_v, wcc=s_cc.n):
                    g.wait_ge(s_v.h, wv)
                    g.wait_ge(s_cc.h, wcc)
                    g.dma_start(out=cs_in,
                                in_=SC[:, C_HS0 + 511:C_HS0 + 512]).then_inc(s_si.h, 16)
                PH.g(_si1)
                s_si.n += 16

                def _agsa(g, wsi=s_si.n, wso=s_so.n):
                    g.wait_ge(s_si.h, wsi)
                    if wso:
                        g.wait_ge(s_so.h, wso)
                    g.collective_compute("AllGather", ALU.bypass, replica_groups=RG,
                                         ins=[cs_in], outs=[cs_out]).then_inc(s_cc.h, 1)
                PH.g(_agsa)
                s_cc.n += 1

                def _so1(g, wcc=s_cc.n, wpv=prev_scan2_v):
                    g.wait_ge(s_cc.h, wcc)
                    if wpv:
                        g.wait_ge(s_v.h, wpv)
                    g.dma_start(out=SC[:, S_CUR0:S_CUR0 + 1], in_=cs_out[0:1, :]).then_inc(s_so.h, 16)
                PH.g(_so1)
                s_so.n += 16
                so1_n = s_so.n

                grp_req[gi] = (s_v, scan1_v)
                emit_gu(range(4, 6))

                def _sc2(v, pg=pg, wso=so1_n):
                    v.wait_ge(s_so.h, wso)
                    v.tensor_scalar(SC[:, S_SEL2:S_SEL2 + 1], SC[:, S_CUR0:S_CUR0 + 1],
                                    WSEL[0:16, 1:2], None, ALU.mult)
                    v.tensor_tensor_scan(SC[:, C_HS2:C_HS2 + 512], SC[:, C_DA:C_DA + 512],
                                         SC[:, C_BXD:C_BXD + 512], SC[:, S_SEL2:S_SEL2 + 1],
                                         ALU.mult, ALU.add)
                    v.tensor_scalar(SC[:, C_HS0:C_HS0 + 512], SC[:, C_HS0:C_HS0 + 512],
                                    WSEL[0:16, 0:1], None, ALU.mult)
                    v.scalar_tensor_tensor(SC[:, C_HS0:C_HS0 + 512], SC[:, C_HS2:C_HS2 + 512],
                                           WSEL[0:16, 1:2], SC[:, C_HS0:C_HS0 + 512],
                                           ALU.mult, ALU.add)
                    v.tensor_copy(HSB[:], SC[:, C_HS0:C_HS0 + 512]).then_inc(s_v.h, 1)
                PH.v(_sc2)
                s_v.n += 1
                scan_v = s_v.n
                prev_scan2_v = scan_v

                if t < nt - 1:
                    def _si2(g, wv=scan_v, wcc=s_cc.n):
                        g.wait_ge(s_v.h, wv)
                        g.wait_ge(s_cc.h, wcc)
                        g.dma_start(out=cs_in, in_=SC[:, C_HS2 + 511:C_HS2 + 512]).then_inc(s_si.h, 16)
                    PH.g(_si2)
                    s_si.n += 16

                    def _agsb(g, wsi=s_si.n, wso=s_so.n):
                        g.wait_ge(s_si.h, wsi)
                        g.wait_ge(s_so.h, wso)
                        g.collective_compute("AllGather", ALU.bypass, replica_groups=RG,
                                             ins=[cs_in], outs=[cs_out]).then_inc(s_cc.h, 1)
                    PH.g(_agsb)
                    s_cc.n += 1

                    def _so2(g, wcc=s_cc.n, wv1=scan1_v):
                        g.wait_ge(s_cc.h, wcc)
                        g.wait_ge(s_v.h, wv1)
                        g.dma_start(out=SC[:, S_PREV1:S_PREV1 + 1], in_=cs_out[1:2, :]).then_inc(s_so.h, 16)
                    PH.g(_so2)
                    s_so.n += 16

            emit_gu(range(6, 12))
            cur_gu_t = s_t.n
            hh_v = s_v.n
            if t + 1 < nt:
                emit_front(t + 1, whh=hh_v)
            for w in range(4):
                gi, pg, req = next_group()

                def _mm(te, req=req, pg=pg, w=w, ws=scan_v):
                    if req is not None:
                        te.wait_ge(req[0].h, req[1])
                    te.wait_ge(s_v.h, ws)
                    i = None
                    for cl in range(4):
                        c = w * 4 + cl
                        te.matmul(pg[:, cl * 512:(cl + 1) * 512], CBF[:, c * 128:(c + 1) * 128],
                                  HSB[:], start=True, stop=False)
                        i = te.matmul(pg[:, cl * 512:(cl + 1) * 512], DBF[:, c * 128:(c + 1) * 128],
                                      XO[:, c * TT:(c + 1) * TT], start=False, stop=True)
                    i.then_inc(s_t.h, 1)
                PH.t(_mm)
                s_t.n += 1
                wv_t = s_t.n

                def _ev(v, pg=pg, w=w, wv=wv_t):
                    v.wait_ge(s_t.h, wv)
                    i = None
                    for cl in range(4):
                        c = w * 4 + cl
                        i = v.tensor_tensor(SPY[:, c * TT:(c + 1) * TT], pg[:, cl * 512:(cl + 1) * 512],
                                            GS[:, c * TT:(c + 1) * TT], ALU.mult)
                    i.then_inc(s_v.h, 1)
                PH.v(_ev)
                s_v.n += 1
                grp_req[gi] = (s_v, s_v.n)
            ybf_v = s_v.n
            prev_y_t_new = s_t.n
            if t + 1 < nt:
                emit_front2(t + 1, cur_gu_t)

            def _outrdy(te, r0=grp_req[0], r1=grp_req[1]):
                if r0 is not None:
                    te.wait_ge(r0[0].h, r0[1])
                if r1 is not None:
                    te.wait_ge(r1[0].h, r1[1])
            PH.t(_outrdy)

            for kg in range(4):
                buf, wn = emit_slab(wov[kg], "wo")

                def _mm(te, buf=buf, wn=wn, kg=kg):
                    te.wait_ge(wn[0].h, wn[1])
                    i = None
                    for kl in range(4):
                        k = kg * 4 + kl
                        for m in range(8):
                            pg = pgA if m < 4 else pgB
                            i = te.matmul(pg[:, (m % 4) * 512:(m % 4) * 512 + 512],
                                          buf[:, kl * 1024 + m * 128:kl * 1024 + m * 128 + 128],
                                          SPY[:, k * TT:(k + 1) * TT],
                                          start=(k == 0), stop=False)
                    i.then_inc(s_t.h, 1)
                PH.t(_mm)
                s_t.n += 1
                slab_done()
            for kg in range(6):
                buf, wn = emit_slab(dnv[kg], "dn")

                def _mm(te, buf=buf, wn=wn, kg=kg, wh=hh_v):
                    te.wait_ge(wn[0].h, wn[1])
                    if kg == 0:
                        te.wait_ge(s_v.h, wh)
                    i = None
                    for kl in range(4):
                        k = kg * 4 + kl
                        for m in range(8):
                            pg = pgA if m < 4 else pgB
                            i = te.matmul(pg[:, (m % 4) * 512:(m % 4) * 512 + 512],
                                          buf[:, kl * 1024 + m * 128:kl * 1024 + m * 128 + 128],
                                          HH[:, k * TT:(k + 1) * TT],
                                          start=False, stop=(k == 23))
                    i.then_inc(s_t.h, 1)
                PH.t(_mm)
                s_t.n += 1
                slab_done()
            out_t = s_t.n

            def _oev(v, w=out_t, t=t):
                v.wait_ge(s_t.h, w)
                if t > 0:
                    v.wait_ge(s_o.h, 16 * t)
                i = None
                for m in range(8):
                    pg = pgA if m < 4 else pgB
                    i = v.tensor_tensor(XOf[:, m * 512:(m + 1) * 512],
                                        pg[:, (m % 4) * 512:(m % 4) * 512 + 512],
                                        XR[:, m * TT:(m + 1) * TT], ALU.add)
                i.then_inc(s_v.h, 1)
            PH.v(_oev)
            s_v.n += 1
            out_v = s_v.n
            grp_req[0] = (s_v, out_v)
            grp_req[1] = (s_v, out_v)
            if t + 1 < nt:
                emit_r(t + 1)

            def _ost(g, t0=t0, w=out_v):
                g.wait_ge(s_v.h, w)
                dst = out_fm.rearrange("(c p) t -> p c t", p=128)[:, :, t0:t0 + TT]
                g.dma_start(out=dst, in_=XOf.rearrange("p (c t) -> p c t", c=DC)).then_inc(s_o.h, 16)
            PH.g(_ost)
            s_o.n += 16

            prev_ybf_v = ybf_v
            prev_win_t = prev_win_t_new
            prev_y_t = prev_y_t_new
            PH.flush()

        if debug:
            @block.gpsimd
            def _(g):
                g.wait_ge(s_o.h, s_o.n)
                for nm, tn in [("d_r", XR), ("d_xn", TB), ("d_hbf", HBF), ("d_xnbf", XNBF),
                               ("d_xspre", XSPRE), ("d_gs", GS), ("d_spy", SPY),
                               ("d_hh", HH), ("d_sc", SC), ("d_sta", STA)]:
                    s_o.inc(g.dma_start(out=dbg[nm], in_=tn[:]), 16)

        @block.gpsimd
        def _(g):
            g.wait_ge(s_o.h, s_o.n)

    return nc
_PROG_CACHE = {}


def _get_prog(nt):
    if nt not in _PROG_CACHE:
        _PROG_CACHE[nt] = build_program(nt, zigzag=True)
    return _PROG_CACHE[nt]


def _win_slabs(w_in):
    a = np.ascontiguousarray(w_in, np.float32).reshape(8, 128, 8, 512)
    return np.ascontiguousarray(a.transpose(2, 1, 0, 3).reshape(8 * 128, 4096))


def _wd_slabs(wd):
    a = np.ascontiguousarray(wd, np.float32).reshape(2, 8, 128, 4, 512)
    return np.ascontiguousarray(a.transpose(3, 0, 2, 1, 4).reshape(8 * 128, 4096))


def _wout_slabs(w_out):
    a = np.ascontiguousarray(w_out, np.float32).reshape(4, 4, 128, 1024)
    return np.ascontiguousarray(a.transpose(0, 2, 1, 3).reshape(4 * 128, 4096))


def _gu_slabs(w_gate, w_up):
    ag = np.ascontiguousarray(np.asarray(w_gate, np.int8).T).reshape(8, 128, 12, 256)
    au = np.ascontiguousarray(np.asarray(w_up, np.int8).T).reshape(8, 128, 12, 256)
    s = np.stack([ag, au], axis=3)
    return np.ascontiguousarray(s.transpose(2, 1, 0, 3, 4).reshape(12 * 128, 4096))


def _dn_slabs(w_down):
    a = np.ascontiguousarray(np.asarray(w_down, np.int8).T).reshape(6, 4, 128, 1024)
    return np.ascontiguousarray(a.transpose(0, 2, 1, 3).reshape(6 * 128, 4096))


def _host_prep(L0_attn_ln_gamma, L0_attn_ln_beta, L0_attn_W_in, L0_attn_conv_w,
               L0_attn_conv_b, L0_attn_A_log, L0_attn_B_mat, L0_attn_C_mat,
               L0_attn_D_vec, L0_attn_Wd, L0_attn_bd, L0_attn_W_out,
               L0_ffn_w_gate, L0_ffn_w_up, L0_ffn_w_down):
    cw = np.asarray(L0_attn_conv_w, np.float32).reshape(DI, 4)
    convd = np.zeros((4 * 128, DIC * 128), np.float32)
    for w in range(4):
        for cl in range(4):
            c = w * 4 + cl
            for k in range(4):
                blk = convd[w * 128:(w + 1) * 128, (cl * 4 + k) * 128:(cl * 4 + k + 1) * 128]
                np.fill_diagonal(blk, cw[c * 128:(c + 1) * 128, k])
    dvec = np.asarray(L0_attn_D_vec, np.float32)
    ddiag = np.zeros((128, DIC * 128), np.float32)
    for c in range(DIC):
        np.fill_diagonal(ddiag[:, c * 128:(c + 1) * 128], dvec[c * 128:(c + 1) * 128])
    return dict(
        win=_win_slabs(L0_attn_W_in),
        wd=_wd_slabs(L0_attn_Wd),
        wout=_wout_slabs(L0_attn_W_out),
        gus=_gu_slabs(L0_ffn_w_gate, L0_ffn_w_up),
        dns=_dn_slabs(L0_ffn_w_down),
        convd=convd, ddiag=ddiag,
        cpre=np.ascontiguousarray(np.asarray(L0_attn_C_mat, np.float32).T),
        bpre=np.ascontiguousarray(np.asarray(L0_attn_B_mat, np.float32).T),
        convb=np.ascontiguousarray(np.asarray(L0_attn_conv_b, np.float32).reshape(DIC, 128).T),
        bdb=np.ascontiguousarray(np.asarray(L0_attn_bd, np.float32).reshape(DIC, 128).T),
        gamv=np.ascontiguousarray(np.asarray(L0_attn_ln_gamma, np.float32).reshape(DC, 128).T),
        betv=np.ascontiguousarray(np.asarray(L0_attn_ln_beta, np.float32).reshape(DC, 128).T),
        alog=np.ascontiguousarray(np.asarray(L0_attn_A_log, np.float32).reshape(1, NS)),
        cvecd=np.repeat(np.array([[EPS, -0.5, -1.0, QS, 1.0 + EPS, EPS * EPS]], np.float32), 128, axis=0),
        wseld=np.zeros((128, 2), np.float32),
    )


def kernel(L0_attn_x, L0_attn_ln_gamma, L0_attn_ln_beta, L0_attn_W_in,
           L0_attn_conv_w, L0_attn_conv_b, L0_attn_A_log, L0_attn_B_mat,
           L0_attn_C_mat, L0_attn_D_vec, L0_attn_Wd, L0_attn_bd,
           L0_attn_W_out, L0_ffn_w_gate, L0_ffn_w_up, L0_ffn_w_down):
    x = np.asarray(L0_attn_x)
    B, L, _ = x.shape
    ntg = L // TT
    nt = ntg // 2

    shared = _host_prep(L0_attn_ln_gamma, L0_attn_ln_beta, L0_attn_W_in,
                        L0_attn_conv_w, L0_attn_conv_b, L0_attn_A_log,
                        L0_attn_B_mat, L0_attn_C_mat, L0_attn_D_vec,
                        L0_attn_Wd, L0_attn_bd, L0_attn_W_out,
                        L0_ffn_w_gate, L0_ffn_w_up, L0_ffn_w_down)

    in_maps = []
    for c in range(8):
        b, par = c // 2, c % 2
        m = dict(shared)
        xt = x[b % B].T.astype(np.float32).reshape(D, ntg, TT)
        m["x_fm"] = np.ascontiguousarray(xt[:, par::2, :].reshape(D, nt * TT))
        ws = np.zeros((128, 2), np.float32)
        ws[:, par] = 1.0
        m["wseld"] = ws
        in_maps.append(m)

    nc = _get_prog(nt)
    res = run_bass_kernel_spmd(nc, in_maps, core_ids=list(range(8)))
    out = np.empty_like(x)
    for b in range(B):
        ot = np.empty((D, ntg, TT), np.float32)
        ot[:, 0::2, :] = res.results[2 * b]["out_fm"].reshape(D, nt, TT)
        ot[:, 1::2, :] = res.results[2 * b + 1]["out_fm"].reshape(D, nt, TT)
        out[b] = ot.reshape(D, L).T
    return out


# version 18 (final)
# speedup vs baseline: 1.1682x; 1.1619x over previous
import contextlib
import numpy as np

import concourse.bass as bass
import concourse.mybir as mybir
from concourse.bass_utils import run_bass_kernel_spmd

F32 = mybir.dt.float32
BF16 = mybir.dt.bfloat16
I32 = mybir.dt.int32
I8 = mybir.dt.int8
AF = mybir.ActivationFunctionType
ALU = mybir.AluOpType

D, DC = 1024, 8
DI, DIC = 2048, 16
H, HC = 3072, 24
NS = 16
TT = 512
EPS = 1e-5
QS = 1.0 / 127.0
QS2 = QS * QS

MU, MSQ, M2, R0, R1 = range(5)
VAR0, LA = MSQ, M2
DELTA = MU
C_DA, C_BXD, C_HS0, C_HS1, C_DB = 0, 512, 1024, 1536, 1536


class Sem:
    def __init__(self, stack, nc, name):
        self.h = stack.enter_context(nc.semaphore(name))
        self.n = 0

    def inc(self, inst, k=1):
        inst.then_inc(self.h, k)
        self.n += k


def build_program(nt, zigzag=False, debug=False):
    nc = bass.Bass("TRN2", target_bir_lowering=False, detect_race_conditions=False)
    NT = nt * TT

    x_fm = nc.dram_tensor("x_fm", [D, NT], F32, kind="ExternalInput").ap()
    win = nc.dram_tensor("win", [8 * 128, 4096], F32, kind="ExternalInput").ap()
    wd = nc.dram_tensor("wd", [8 * 128, 4096], F32, kind="ExternalInput").ap()
    wout = nc.dram_tensor("wout", [4 * 128, 4096], F32, kind="ExternalInput").ap()
    gus = nc.dram_tensor("gus", [12 * 128, 4096], I8, kind="ExternalInput").ap()
    dns = nc.dram_tensor("dns", [6 * 128, 4096], I8, kind="ExternalInput").ap()
    convd = nc.dram_tensor("convd", [4 * 128, DIC * 128], F32, kind="ExternalInput").ap()
    ddiag = nc.dram_tensor("ddiag", [128, DIC * 128], F32, kind="ExternalInput").ap()
    cpre = nc.dram_tensor("cpre", [NS, DI], F32, kind="ExternalInput").ap()
    bpre = nc.dram_tensor("bpre", [DI, NS], F32, kind="ExternalInput").ap()
    convb = nc.dram_tensor("convb", [128, DIC], F32, kind="ExternalInput").ap()
    bdb = nc.dram_tensor("bdb", [128, DIC], F32, kind="ExternalInput").ap()
    gamv = nc.dram_tensor("gamv", [128, DC], F32, kind="ExternalInput").ap()
    betv = nc.dram_tensor("betv", [128, DC], F32, kind="ExternalInput").ap()
    alog = nc.dram_tensor("alog", [1, NS], F32, kind="ExternalInput").ap()
    cvecd = nc.dram_tensor("cvecd", [128, 6], F32, kind="ExternalInput").ap()
    wseld = nc.dram_tensor("wseld", [128, 2], F32, kind="ExternalInput").ap()
    ch_in = nc.dram_tensor("ch_in", [1, 6144], BF16, kind="Internal").ap()
    ch_out = nc.dram_tensor("ch_out", [2, 6144], BF16, kind="Internal").ap()
    cs_in = nc.dram_tensor("cs_in", [1, NS], F32, kind="Internal").ap()
    cs_out = nc.dram_tensor("cs_out", [2, NS], F32, kind="Internal").ap()
    RG = [[0, 1], [2, 3], [4, 5], [6, 7]]
    out_fm = nc.dram_tensor("out_fm", [D, NT], F32, kind="ExternalOutput").ap()
    dbg = {}
    if debug:
        for nm, shp, dt in [("d_r", [128, DC * TT], F32), ("d_xn", [128, DC * TT], F32),
                            ("d_hbf", [128, DC * TT], BF16), ("d_xnbf", [128, DC * TT], BF16),
                            ("d_xspre", [128, DIC * 516], BF16),
                            ("d_gs", [128, DIC * TT], F32), ("d_spy", [128, DIC * TT], BF16),
                            ("d_hh", [128, HC * TT], BF16), ("d_sc", [16, 2576], F32),
                            ("d_sta", [1, 5 * TT], F32)]:
            dbg[nm] = nc.dram_tensor(nm, shp, dt, kind="ExternalOutput").ap()

    s_win = nc.dram_tensor("s_win", [8 * 128, 4096], BF16, kind="Internal").ap()
    s_conv = nc.dram_tensor("s_conv", [4 * 128, 2048], BF16, kind="Internal").ap()
    s_wd = nc.dram_tensor("s_wd", [8 * 128, 4096], BF16, kind="Internal").ap()
    s_wo = nc.dram_tensor("s_wo", [4 * 128, 4096], BF16, kind="Internal").ap()
    s_gu = nc.dram_tensor("s_gu", [12 * 128, 4096], BF16, kind="Internal").ap()
    s_dn = nc.dram_tensor("s_dn", [6 * 128, 4096], BF16, kind="Internal").ap()

    stck = contextlib.ExitStack()
    with stck:
        XR = stck.enter_context(nc.sbuf_tensor("XR", [128, DC * TT], F32))
        TB = stck.enter_context(nc.sbuf_tensor("TB", [128, DC * TT], F32))
        XNBF = stck.enter_context(nc.sbuf_tensor("XNBF", [128, DC * TT], BF16))
        HBF = stck.enter_context(nc.sbuf_tensor("HBF", [128, DC * TT], BF16))
        XSPRE = stck.enter_context(nc.sbuf_tensor("XSPRE", [128, DIC * 516], BF16))
        HH = stck.enter_context(nc.sbuf_tensor("HH", [128, HC * TT], BF16))
        GS = stck.enter_context(nc.sbuf_tensor("GS", [128, DIC * TT], F32))
        SPY = stck.enter_context(nc.sbuf_tensor("SPY", [128, DIC * TT], BF16))
        XO = stck.enter_context(nc.sbuf_tensor("XO", [128, DIC * TT], BF16))
        WP = [stck.enter_context(nc.sbuf_tensor(f"WP{i}", [128, 4096], BF16)) for i in range(3)]
        SC = stck.enter_context(nc.sbuf_tensor("SC", [16, 2576], F32))
        HPK = stck.enter_context(nc.sbuf_tensor("HPK", [128, 48], BF16))
        HIN = stck.enter_context(nc.sbuf_tensor("HIN", [128, 96], BF16))
        HPR = stck.enter_context(nc.sbuf_tensor("HPR", [128, 48], BF16))
        HTMP = stck.enter_context(nc.sbuf_tensor("HTMP", [128, 48], BF16))
        WSEL = stck.enter_context(nc.sbuf_tensor("WSEL", [128, 2], F32))
        HSB = stck.enter_context(nc.sbuf_tensor("HSB", [16, 512], BF16))
        STA = stck.enter_context(nc.sbuf_tensor("STA", [1, 5 * TT], F32))
        GAM = stck.enter_context(nc.sbuf_tensor("GAM", [128, DC], F32))
        BET = stck.enter_context(nc.sbuf_tensor("BET", [128, DC], F32))
        CVB = stck.enter_context(nc.sbuf_tensor("CVB", [128, DIC], F32))
        BDBs = stck.enter_context(nc.sbuf_tensor("BDBs", [128, DIC], F32))
        CBF = stck.enter_context(nc.sbuf_tensor("CBF", [16, DI], BF16))
        DBF = stck.enter_context(nc.sbuf_tensor("DBF", [128, DIC * 128], BF16))
        BBF = stck.enter_context(nc.sbuf_tensor("BBF", [128, DIC * NS], BF16))
        AROW = stck.enter_context(nc.sbuf_tensor("AROW", [1, NS], F32))
        ONE1 = stck.enter_context(nc.sbuf_tensor("ONE1", [1, 128], F32))
        ONEB = stck.enter_context(nc.sbuf_tensor("ONEB", [128, 1], BF16))
        ONEF = stck.enter_context(nc.sbuf_tensor("ONEF", [128, 1], F32))
        ON16 = stck.enter_context(nc.sbuf_tensor("ON16", [1, NS], F32))
        CVEC = stck.enter_context(nc.sbuf_tensor("CVEC", [128, 6], F32))
        pgA = stck.enter_context(nc.psum_tensor("pgA", [128, 2048], F32))
        pgB = stck.enter_context(nc.psum_tensor("pgB", [128, 2048], F32))

        s_x = Sem(stck, nc, "s_x")
        s_w = [Sem(stck, nc, "s_w0"), Sem(stck, nc, "s_w1"), Sem(stck, nc, "s_w2")]
        s_t = Sem(stck, nc, "s_t")
        s_v = Sem(stck, nc, "s_v")
        s_s = Sem(stck, nc, "s_s")
        s_o = Sem(stck, nc, "s_o")
        s_cl = [Sem(stck, nc, "s_cl0"), Sem(stck, nc, "s_cl1")]
        s_cs = [Sem(stck, nc, "s_cs0"), Sem(stck, nc, "s_cs1")]
        s_cc = Sem(stck, nc, "s_cc")
        s_k = Sem(stck, nc, "s_k")
        s_hi = Sem(stck, nc, "s_hi")
        s_ho = Sem(stck, nc, "s_ho")
        s_si = Sem(stck, nc, "s_si")
        s_so = Sem(stck, nc, "s_so")
        block = stck.enter_context(nc.Block())

        XOf = XO.ap().bitcast(F32)
        XSPf = XSPRE.ap().bitcast(F32)

        def sta(i):
            return STA[0:1, i * TT:(i + 1) * TT]

        def xsp(c, lo, hi):
            return XSPRE[:, c * 516 + lo:c * 516 + hi]

        class PhaseBuf:
            def __init__(self):
                self.q = {"t": [], "v": [], "s": [], "g": [], "y": []}

            def t(self, f):
                self.q["t"].append(f)

            def v(self, f):
                self.q["v"].append(f)

            def s(self, f):
                self.q["s"].append(f)

            def g(self, f):
                self.q["g"].append(f)

            def y(self, f):
                self.q["y"].append(f)

            def flush(self):
                qs = self.q
                self.q = {"t": [], "v": [], "s": [], "g": [], "y": []}
                if qs["y"]:
                    @block.sync
                    def _(e, fs=qs["y"]):
                        for f in fs:
                            f(e)
                if qs["g"]:
                    @block.gpsimd
                    def _(e, fs=qs["g"]):
                        for f in fs:
                            f(e)
                if qs["t"]:
                    @block.tensor
                    def _(e, fs=qs["t"]):
                        for f in fs:
                            f(e)
                if qs["v"]:
                    @block.vector
                    def _(e, fs=qs["v"]):
                        for f in fs:
                            f(e)
                if qs["s"]:
                    @block.scalar
                    def _(e, fs=qs["s"]):
                        for f in fs:
                            f(e)

        PH = PhaseBuf()

        jobs = []
        winv = s_win.rearrange("(s p) c -> s p c", p=128)
        srcwin = win.rearrange("(s p) c -> s p c", p=128)
        for s in range(8):
            jobs.append((srcwin[s], F32, winv[s], 4096))
        cvv = s_conv.rearrange("(s p) c -> s p c", p=128)
        for w in range(4):
            jobs.append((convd[w * 128:(w + 1) * 128, :], F32, cvv[w], 2048))
        wdv = s_wd.rearrange("(s p) c -> s p c", p=128)
        srcwd = wd.rearrange("(s p) c -> s p c", p=128)
        for s in range(8):
            jobs.append((srcwd[s], F32, wdv[s], 4096))
        guv = s_gu.rearrange("(s p) c -> s p c", p=128)
        srcgu = gus.rearrange("(s p) c -> s p c", p=128)
        for w in range(12):
            jobs.append((srcgu[w], I8, guv[w], 4096))
        wov = s_wo.rearrange("(s p) c -> s p c", p=128)
        srcwo = wout.rearrange("(s p) c -> s p c", p=128)
        for s in range(4):
            jobs.append((srcwo[s], F32, wov[s], 4096))
        dnv = s_dn.rearrange("(s p) c -> s p c", p=128)
        srcdn = dns.rearrange("(s p) c -> s p c", p=128)
        for s in range(6):
            jobs.append((srcdn[s], I8, dnv[s], 4096))

        XOc = XO.ap().bitcast(F32)
        SPYc = SPY.ap().bitcast(F32)

        def _consts(s):
            for dst, src in [(GAM, gamv), (BET, betv), (CVB, convb), (BDBs, bdb)]:
                s.dma_start(out=dst[:], in_=src).then_inc(s_k.h, 16)
            s.dma_start(out=AROW[:], in_=alog).then_inc(s_k.h, 16)
            s.dma_start(out=CVEC[:], in_=cvecd).then_inc(s_k.h, 16)
            s.dma_start(out=WSEL[:], in_=wseld).then_inc(s_k.h, 16)
            s.dma_start(out=XOc[0:16, 0:DI], in_=cpre).then_inc(s_k.h, 16)
            s.dma_start(
                out=SPYc[:, 0:256].rearrange("p (a b) -> p a b", a=DIC),
                in_=bpre.rearrange("(c p) n -> p c n", p=128)).then_inc(s_k.h, 16)
            s.dma_start(out=XOc[:, DI:2 * DI], in_=ddiag).then_inc(s_k.h, 16)
        PH.s(_consts)
        s_k.n += 16 * 10
        consts_lc = s_k.n

        def _arow(s):
            s.wait_ge(s_k.h, consts_lc)
            s.activation(AROW[:], AROW[:], AF.Exp).then_inc(s_s.h, 1)
        PH.s(_arow)
        s_s.n += 1
        arow_sc = s_s.n

        def _cinit(v):
            v.memset(ONE1[:], 1.0)
            v.memset(ONEB[:], 1.0)
            v.memset(ONEF[:], 1.0)
            v.memset(ON16[:], 1.0)
            v.memset(XSPRE[:], 0.0)
            v.memset(SC[:], 0.0)
            v.memset(HPR[:], 0.0)
            v.memset(HIN[:], 0.0)
            v.wait_ge(s_k.h, consts_lc)
            v.wait_ge(s_s.h, arow_sc)
            v.tensor_scalar(AROW[:], AROW[:], -1.0, None, ALU.mult)
            v.tensor_copy(CBF[:], XOc[0:16, 0:DI])
            v.tensor_copy(BBF[:], SPYc[:, 0:256])
            v.tensor_copy(DBF[:], XOc[:, DI:2 * DI]).then_inc(s_v.h, 1)
        PH.v(_cinit)
        s_v.n += 1
        consts_done_v = s_v.n
        cast_counts = []
        store_counts = []
        grp_ready = {}
        _grp_end = {7: "win", 11: "conv", 19: "wd", 31: "gu", 35: "wo", 41: "dn"}
        GSi8 = GS.ap().bitcast(I8)
        for j, (src, dt, dst, wj) in enumerate(jobs):
            par = j % 2
            if dt == I8:
                fstage = GSi8[:, par * 16384:par * 16384 + wj]
            else:
                fstage = GS[:, par * 4096:par * 4096 + wj]
            bstage = HH[:, par * 4096:par * 4096 + wj]
            scl = s_cl[par]
            scs = s_cs[par]

            def _ld(s, src=src, dst_=fstage, j=j, scl=scl):
                if j >= 2:
                    s.wait_ge(s_v.h, cast_counts[j - 2])
                s.dma_start(out=dst_, in_=src).then_inc(scl.h, 16)
            PH.s(_ld)
            scl.n += 16
            lc = scl.n

            def _cast(v, fs=fstage, bs=bstage, j=j, lc=lc, scl=scl, scs=scs):
                v.wait_ge(scl.h, lc)
                if j >= 2:
                    v.wait_ge(scs.h, store_counts[j - 2])
                v.tensor_copy(bs, fs).then_inc(s_v.h, 1)
            PH.v(_cast)
            s_v.n += 1
            cast_counts.append(s_v.n)

            def _st(g, dst=dst, wj=wj, bs=bstage, vc=s_v.n, scs=scs):
                g.wait_ge(s_v.h, vc)
                g.dma_start(out=dst[:, 0:wj], in_=bs[:, 0:wj]).then_inc(scs.h, 16)
            PH.g(_st)
            scs.n += 16
            store_counts.append(scs.n)
            if j in _grp_end:
                grp_ready[_grp_end[j]] = (s_cs[0].n, s_cs[1].n)

        PH.flush()

        grp = [pgA, pgB]
        grp_req = [None, None]
        wave_ctr = [0]

        def next_group():
            gi = wave_ctr[0] % 2
            wave_ctr[0] += 1
            return gi, grp[gi], grp_req[gi]

        slab_ctr = [0]
        slab_free = []

        seen_grp = set()

        def emit_slab(src2, grp, width=4096):
            i = slab_ctr[0]
            slab_ctr[0] += 1
            par = i % 3
            buf = WP[par]
            sw = s_w[par]
            gw = None
            if grp not in seen_grp:
                seen_grp.add(grp)
                gw = grp_ready.get(grp)
            free_at = slab_free[i - 3] if i >= 3 else None

            def _dma(sy, src2=src2, buf=buf, width=width, sw=sw, gw=gw, free_at=free_at):
                if gw is not None:
                    sy.wait_ge(s_cs[0].h, gw[0])
                    sy.wait_ge(s_cs[1].h, gw[1])
                if free_at is not None:
                    sy.wait_ge(s_t.h, free_at)
                sy.dma_start(out=buf[:, 0:width], in_=src2).then_inc(sw.h, 16)
            PH.y(_dma)
            sw.n += 16
            return buf, (sw, sw.n)

        def slab_done():
            slab_free.append(s_t.n)

        FR = {}

        def emit_front(ft, first=False, whh=0):
            t0f = ft * TT

            def _xld(g, t0f=t0f, wcv=prev_conv_t, whp=halo_v if not zigzag else hpk_v_last[0],
                     wc0=(consts_done_v if first else 0)):
                if wc0:
                    g.wait_ge(s_v.h, wc0)
                if wcv:
                    g.wait_ge(s_t.h, wcv)
                if whp:
                    g.wait_ge(s_v.h, whp)
                srcx = x_fm.rearrange("(c p) t -> p c t", p=128)[:, :, t0f:t0f + TT]
                g.dma_start(out=XSPf[:, 0:4096].rearrange("p (c t) -> p c t", c=DC),
                            in_=srcx).then_inc(s_x.h, 16)
            PH.g(_xld)
            s_x.n += 16
            x_ld = s_x.n

            def _xsq(s, x_ld=x_ld, w8=(whh or consts_done_v)):
                s.wait_ge(s_x.h, x_ld)
                s.wait_ge(s_v.h, w8)
                s.activation(TB[:], XSPf[:, 0:4096], AF.Square).then_inc(s_s.h, 1)
            PH.s(_xsq)
            s_s.n += 1
            xsq_s = s_s.n

            gi1, pg1, req1 = next_group()

            def _stats(te, x_ld=x_ld, xsq_s=xsq_s, req=req1, pg=pg1):
                te.wait_ge(s_x.h, x_ld)
                if req is not None:
                    te.wait_ge(req[0].h, req[1])
                for c in range(DC):
                    te.matmul(pg[0:1, 0:512], ONEF[:], XSPf[:, c * TT:(c + 1) * TT],
                              start=(c == 0), stop=(c == DC - 1))
                te.wait_ge(s_s.h, xsq_s)
                i = None
                for c in range(DC):
                    i = te.matmul(pg[0:1, 512:1024], ONEF[:], TB[:, c * TT:(c + 1) * TT],
                                  start=(c == 0), stop=(c == DC - 1))
                i.then_inc(s_t.h, 1)
            PH.t(_stats)
            s_t.n += 1
            stats_t = s_t.n

            def _tv1(v, w=stats_t, pg=pg1):
                v.wait_ge(s_t.h, w)
                v.tensor_scalar(sta(MU), pg[0:1, 0:512], 1.0 / D, None, ALU.mult)
                v.tensor_scalar(sta(MSQ), pg[0:1, 512:1024], 1.0 / D, None, ALU.mult)
                v.tensor_tensor(sta(M2), sta(MU), sta(MU), ALU.mult)
                v.tensor_tensor(sta(VAR0), sta(MSQ), sta(M2), ALU.subtract).then_inc(s_v.h, 1)
            PH.v(_tv1)
            s_v.n += 1
            st_v = s_v.n
            grp_req[gi1] = (s_v, st_v)

            def _ts1(s, w=st_v):
                s.wait_ge(s_v.h, w)
                s.activation(sta(LA), sta(VAR0), AF.Ln, bias=CVEC[0:1, 0:1])
                s.activation(sta(R0), sta(LA), AF.Exp, scale=CVEC[0:1, 1:2])
                s.activation(sta(LA), sta(VAR0), AF.Ln, bias=CVEC[0:1, 5:6], scale=CVEC[0:1, 4:5])
                s.activation(sta(R1), sta(LA), AF.Exp, scale=CVEC[0:1, 1:2]).then_inc(s_s.h, 1)
            PH.s(_ts1)
            s_s.n += 1
            st_s = s_s.n
            FR[ft] = dict(x_ld=x_ld, st_s=st_s)

        def emit_front2(ft, cur_gu_t):
            st_s = FR[ft]["st_s"]
            gi2, pg2, req2 = next_group()

            def _bc(te, w=st_s, req=req2, pg=pg2):
                te.wait_ge(s_s.h, w)
                if req is not None:
                    te.wait_ge(req[0].h, req[1])
                te.matmul(pg[:, 0:512], ONE1[:], sta(MU), start=True, stop=True)
                te.matmul(pg[:, 512:1024], ONE1[:], sta(R0), start=True, stop=True)
                te.matmul(pg[:, 1024:1536], ONE1[:], sta(R1), start=True, stop=True).then_inc(s_t.h, 1)
            PH.t(_bc)
            s_t.n += 1
            bc_t = s_t.n

            def _ln(v, w=bc_t, wgu=cur_gu_t, pg=pg2):
                v.wait_ge(s_t.h, w)
                v.wait_ge(s_t.h, wgu)
                for c in range(DC):
                    v.tensor_tensor(TB[:, c * TT:(c + 1) * TT], XSPf[:, c * TT:(c + 1) * TT],
                                    pg[:, 0:512], ALU.subtract)
                for c in range(DC):
                    v.tensor_tensor(GS[:, 4096 + c * 512:4096 + (c + 1) * 512],
                                    TB[:, c * TT:(c + 1) * TT], pg[:, 1024:1536], ALU.mult)
                for c in range(DC):
                    v.tensor_tensor(TB[:, c * TT:(c + 1) * TT], TB[:, c * TT:(c + 1) * TT],
                                    pg[:, 512:1024], ALU.mult)
                v.tensor_copy(XNBF[:], TB[:]).then_inc(s_v.h, 1)
            PH.v(_ln)
            s_v.n += 1
            ln_v = s_v.n
            grp_req[gi2] = (s_v, ln_v)

            def _hbf(s, w=ln_v, wwin=prev_win_t):
                s.wait_ge(s_v.h, w)
                if wwin:
                    s.wait_ge(s_t.h, wwin)
                i = None
                for c in range(DC):
                    i = s.activation(HBF[:, c * TT:(c + 1) * TT], GS[:, 4096 + c * 512:4096 + (c + 1) * 512],
                                     AF.Identity, bias=BET[:, c:c + 1], scale=GAM[:, c:c + 1])
                i.then_inc(s_s.h, 1)
            PH.s(_hbf)
            s_s.n += 1
            FR[ft]["ln_v"] = ln_v
            FR[ft]["hbf_s"] = s_s.n

        def emit_r(ft):
            def _r(v):
                v.tensor_tensor(XR[:], XSPf[:, 0:4096], TB[:], ALU.add).then_inc(s_v.h, 1)
            PH.v(_r)
            s_v.n += 1
            FR[ft]["r_v"] = s_v.n
            if not zigzag:
                def _hrest(v, z=(ft == 0)):
                    if z:
                        v.memset(HPK[:], 0.0)
                    v.tensor_copy(
                        XSPRE.ap().rearrange("p (c t) -> p c t", c=DIC)[:, :, 0:3],
                        HPK.ap().rearrange("p (c t) -> p c t", c=DIC)).then_inc(s_v.h, 1)
                PH.v(_hrest)
                s_v.n += 1
            if not zigzag:
                FR[ft]["halo_v"] = s_v.n

        hpk_v_last = [0]
        halo_v = 0
        halo_used_v = 0
        prev_ybf_v = 0
        prev_scan2_v = 0
        prev_hpr_v = 0
        prev_conv_t = 0
        prev_win_t = 0
        prev_y_t = 0

        for t in range(nt):
            t0 = t * TT

            if t == 0:
                emit_front(0, first=True)
                emit_front2(0, 0)
                emit_r(0)
            fr = FR[t]
            x_ld = fr["x_ld"]
            ln_v = fr["ln_v"]
            hbf_s = fr["hbf_s"]
            r_v = fr["r_v"]

            win_evac = []
            for mg in range(8):
                buf, wn = emit_slab(winv[mg], "win")
                gi, pg, req = next_group()

                def _mm(te, buf=buf, wn=wn, req=req, pg=pg, w=hbf_s):
                    te.wait_ge(wn[0].h, wn[1])
                    if req is not None:
                        te.wait_ge(req[0].h, req[1])
                    te.wait_ge(s_s.h, w)
                    i = None
                    for ml in range(4):
                        for k in range(8):
                            i = te.matmul(pg[:, ml * 512:(ml + 1) * 512],
                                          buf[:, k * 512 + ml * 128:k * 512 + ml * 128 + 128],
                                          HBF[:, k * TT:(k + 1) * TT],
                                          start=(k == 0), stop=(k == 7))
                    i.then_inc(s_t.h, 1)
                PH.t(_mm)
                s_t.n += 1
                slab_done()
                wv_t = s_t.n

                def _ev(s, pg=pg, mg=mg, w=wv_t, whal=halo_v, wybf=prev_ybf_v, wr=r_v):
                    s.wait_ge(s_t.h, w)
                    i = None
                    if mg < 4:
                        s.wait_ge(s_v.h, wr)
                        if whal:
                            s.wait_ge(s_v.h, whal)
                        for ml in range(4):
                            c = mg * 4 + ml
                            i = s.copy(xsp(c, 3, 515), pg[:, ml * 512:(ml + 1) * 512])
                    else:
                        if mg == 4 and wybf:
                            s.wait_ge(s_v.h, wybf)
                        for ml in range(4):
                            c = (mg - 4) * 4 + ml
                            i = s.activation(GS[:, c * TT:(c + 1) * TT],
                                             pg[:, ml * 512:(ml + 1) * 512], AF.Silu)
                    i.then_inc(s_s.h, 1)
                PH.s(_ev)
                s_s.n += 1
                grp_req[gi] = (s_s, s_s.n)
                win_evac.append(s_s.n)
            prev_win_t_new = s_t.n

            if zigzag:
                def _hpk(v, w=win_evac[3], wcv=prev_conv_t):
                    v.wait_ge(s_s.h, w)
                    if wcv:
                        v.wait_ge(s_t.h, wcv)
                    v.tensor_copy(HPK.ap().rearrange("p (c t) -> p c t", c=DIC),
                                  XSPRE.ap().rearrange("p (c t) -> p c t", c=DIC)[:, :, 512:515]).then_inc(s_v.h, 1)
                PH.v(_hpk)
                s_v.n += 1
                hpk_v = s_v.n
                hpk_v_last[0] = hpk_v

                def _hin(g, wv=hpk_v, wcc=s_cc.n, who=s_ho.n):
                    g.wait_ge(s_v.h, wv)
                    if wcc:
                        g.wait_ge(s_cc.h, wcc)
                    g.dma_start(out=ch_in.rearrange("a (p c) -> a p c", p=128)[0], in_=HPK[:]).then_inc(s_hi.h, 16)
                PH.g(_hin)
                s_hi.n += 16

                def _agh(g, whi=s_hi.n, who=s_ho.n):
                    g.wait_ge(s_hi.h, whi)
                    if who:
                        g.wait_ge(s_ho.h, who)
                    g.collective_compute("AllGather", ALU.bypass, replica_groups=RG,
                                         ins=[ch_in], outs=[ch_out]).then_inc(s_cc.h, 1)
                PH.g(_agh)
                s_cc.n += 1

                def _hout(g, wcc=s_cc.n, wprv=prev_hpr_v):
                    g.wait_ge(s_cc.h, wcc)
                    if wprv:
                        g.wait_ge(s_v.h, wprv)
                    g.dma_start(out=HIN[:, 0:48], in_=ch_out.rearrange("a (p c) -> a p c", p=128)[0]).then_inc(s_ho.h, 16)
                    g.dma_start(out=HIN[:, 48:96], in_=ch_out.rearrange("a (p c) -> a p c", p=128)[1]).then_inc(s_ho.h, 16)
                PH.g(_hout)
                s_ho.n += 32

                def _husev(v, who=s_ho.n, wcv=prev_conv_t, wr=r_v):
                    v.wait_ge(s_ho.h, who)
                    v.wait_ge(s_v.h, wr)
                    if wcv:
                        v.wait_ge(s_t.h, wcv)
                    v.tensor_scalar(HTMP[:], HPR[:], WSEL[:, 0:1], None, ALU.mult)
                    v.scalar_tensor_tensor(
                        XSPRE.ap().rearrange("p (c t) -> p c t", c=DIC)[:, :, 0:3],
                        HIN[:, 0:48].rearrange("p (c t) -> p c t", c=DIC),
                        WSEL[:, 1:2], HTMP.ap().rearrange("p (c t) -> p c t", c=DIC),
                        ALU.mult, ALU.add)
                    v.tensor_copy(HPR[:], HIN[:, 48:96]).then_inc(s_v.h, 1)
                PH.v(_husev)
                s_v.n += 1
                halo_used_v = s_v.n
                prev_hpr_v = halo_used_v

            for w in range(4):
                buf, wn = emit_slab(cvv[w], "conv", width=2048)
                gi, pg, req = next_group()

                def _mm(te, buf=buf, wn=wn, req=req, pg=pg, w=w, wev=win_evac[w],
                        whv=(halo_used_v if zigzag else FR[t].get("halo_v", 0))):
                    te.wait_ge(wn[0].h, wn[1])
                    if req is not None:
                        te.wait_ge(req[0].h, req[1])
                    te.wait_ge(s_s.h, wev)
                    if whv:
                        te.wait_ge(s_v.h, whv)
                    i = None
                    for cl in range(4):
                        c = w * 4 + cl
                        for k in range(4):
                            i = te.matmul(pg[:, cl * 512:(cl + 1) * 512],
                                          buf[:, (cl * 4 + k) * 128:(cl * 4 + k + 1) * 128],
                                          xsp(c, k, k + 512),
                                          start=(k == 0), stop=(k == 3))
                    i.then_inc(s_t.h, 1)
                PH.t(_mm)
                s_t.n += 1
                slab_done()
                wv_t = s_t.n

                def _ev(s, pg=pg, w=w, wv=wv_t, wo=16 * t):
                    s.wait_ge(s_t.h, wv)
                    if w == 0 and wo:
                        s.wait_ge(s_o.h, wo)
                    i = None
                    for cl in range(4):
                        c = w * 4 + cl
                        i = s.activation(XO[:, c * TT:(c + 1) * TT], pg[:, cl * 512:(cl + 1) * 512],
                                         AF.Silu, bias=CVB[:, c:c + 1])
                    i.then_inc(s_s.h, 1)
                PH.s(_ev)
                s_s.n += 1
                grp_req[gi] = (s_s, s_s.n)
            conv_s = s_s.n
            conv_t = s_t.n

            prev_conv_t = conv_t
            if not zigzag:
                def _halo(v, w=conv_t):
                    v.wait_ge(s_t.h, w)
                    v.tensor_copy(
                        HPK.ap().rearrange("p (c t) -> p c t", c=DIC),
                        XSPRE.ap().rearrange("p (c t) -> p c t", c=DIC)[:, :, 512:515]).then_inc(s_v.h, 1)
                PH.v(_halo)
                s_v.n += 1
                hpk_v_last[0] = s_v.n

            for mg in range(4):
                gi, pg, req = next_group()
                for kh in range(2):
                    buf, wn = emit_slab(wdv[mg * 2 + kh], "wd")

                    def _mm(te, buf=buf, wn=wn, req=req, pg=pg, kh=kh, wc=conv_s):
                        te.wait_ge(wn[0].h, wn[1])
                        if kh == 0:
                            if req is not None:
                                te.wait_ge(req[0].h, req[1])
                            te.wait_ge(s_s.h, wc)
                        i = None
                        for ml in range(4):
                            for kl in range(8):
                                k = kh * 8 + kl
                                i = te.matmul(pg[:, ml * 512:(ml + 1) * 512],
                                              buf[:, kl * 512 + ml * 128:kl * 512 + ml * 128 + 128],
                                              XO[:, k * TT:(k + 1) * TT],
                                              start=(k == 0), stop=(k == 15))
                        i.then_inc(s_t.h, 1)
                    PH.t(_mm)
                    s_t.n += 1
                    slab_done()
                wv_t = s_t.n

                def _ev(s, pg=pg, mg=mg, w=wv_t):
                    s.wait_ge(s_t.h, w)
                    for ml in range(4):
                        c = mg * 4 + ml
                        s.activation(pg[:, ml * 512:(ml + 1) * 512], pg[:, ml * 512:(ml + 1) * 512],
                                     AF.Exp, bias=BDBs[:, c:c + 1])
                    i = None
                    for ml in range(4):
                        c = mg * 4 + ml
                        i = s.activation(SPY[:, c * TT:(c + 1) * TT], pg[:, ml * 512:(ml + 1) * 512],
                                         AF.Ln, bias=1.0)
                    i.then_inc(s_s.h, 1)
                PH.s(_ev)
                s_s.n += 1
                grp_req[gi] = (s_s, s_s.n)
            sp_s = s_s.n

            gu_stt = []

            def emit_gu(ws):
                for w in ws:
                    buf, wn = emit_slab(guv[w], "gu")
                    gi, pg, req = next_group()

                    def _mm(te, buf=buf, wn=wn, req=req, w=w, pg=pg, wln=ln_v):
                        te.wait_ge(wn[0].h, wn[1])
                        if req is not None:
                            te.wait_ge(req[0].h, req[1])
                        if w == 0:
                            te.wait_ge(s_v.h, wln)
                        i = None
                        for gl in range(4):
                            for k in range(8):
                                i = te.matmul(pg[:, gl * 512:(gl + 1) * 512],
                                              buf[:, k * 512 + gl * 128:k * 512 + gl * 128 + 128],
                                              XNBF[:, k * TT:(k + 1) * TT],
                                              start=(k == 0), stop=(k == 7))
                        i.then_inc(s_t.h, 1)
                    PH.t(_mm)
                    s_t.n += 1
                    slab_done()
                    wv_t = s_t.n
                    reg = (w % 4) * 1024

                    def _ev(s, pg=pg, wv=wv_t, w=w, reg=reg):
                        s.wait_ge(s_t.h, wv)
                        if w >= 4:
                            s.wait_ge(s_v.h, gu_stt[w - 4])
                        s.activation(TB[:, reg:reg + 512], pg[:, 0:512], AF.Silu, scale=CVEC[:, 3:4])
                        s.activation(TB[:, reg + 512:reg + 1024], pg[:, 512:1024], AF.Silu, scale=CVEC[:, 3:4]).then_inc(s_s.h, 1)
                    PH.s(_ev)
                    s_s.n += 1
                    gsil_s = s_s.n

                    def _hh(v, pg=pg, w=w, wg=gsil_s, reg=reg):
                        v.wait_ge(s_s.h, wg)
                        v.scalar_tensor_tensor(HH[:, (2 * w) * TT:(2 * w + 1) * TT], pg[:, 1024:1536],
                                               QS2, TB[:, reg:reg + 512], ALU.mult, ALU.mult)
                        i = v.scalar_tensor_tensor(HH[:, (2 * w + 1) * TT:(2 * w + 2) * TT], pg[:, 1536:2048],
                                                   QS2, TB[:, reg + 512:reg + 1024], ALU.mult, ALU.mult)
                        i.then_inc(s_v.h, 1)
                    PH.v(_hh)
                    s_v.n += 1
                    gu_stt.append(s_v.n)
                    grp_req[gi] = (s_v, s_v.n)

            emit_gu(range(4))
            gi, pg, req = next_group()

            def _dl(te, req=req, pg=pg, w=sp_s):
                if req is not None:
                    te.wait_ge(req[0].h, req[1])
                te.wait_ge(s_s.h, w)
                i = None
                for c in range(DIC):
                    i = te.matmul(pg[0:1, 0:512], ONEB[:], SPY[:, c * TT:(c + 1) * TT],
                                  start=(c == 0), stop=(c == DIC - 1))
                i.then_inc(s_t.h, 1)
            PH.t(_dl)
            s_t.n += 1
            dl_t = s_t.n

            def _dlv(v, pg=pg, w=dl_t):
                v.wait_ge(s_t.h, w)
                v.tensor_scalar(sta(DELTA), pg[0:1, 0:512], 1.0 / DI, None, ALU.mult).then_inc(s_v.h, 1)
            PH.v(_dlv)
            s_v.n += 1
            dl_v = s_v.n

            def _ssm(te, pg=pg, w=dl_v):
                te.wait_ge(s_v.h, w)
                te.matmul(pg[0:16, 512:1024], AROW[:], sta(DELTA), start=True, stop=True)
                te.matmul(pg[0:16, 1024:1536], ON16[:], sta(DELTA), start=True, stop=True)
                i = None
                for c in range(DIC):
                    i = te.matmul(pg[0:16, 1536:2048], BBF[:, c * NS:(c + 1) * NS],
                                  XO[:, c * TT:(c + 1) * TT], start=(c == 0), stop=(c == DIC - 1))
                i.then_inc(s_t.h, 1)
            PH.t(_ssm)
            s_t.n += 1
            ssm_t = s_t.n

            def _da(s, pg=pg, w=ssm_t):
                s.wait_ge(s_t.h, w)
                s.activation(SC[:, C_DA:C_DA + 512], pg[0:16, 512:1024], AF.Exp).then_inc(s_s.h, 1)
            PH.s(_da)
            s_s.n += 1
            da_s = s_s.n

            if not zigzag:
                hs_cur = C_HS0 if t % 2 == 0 else C_HS1
                hs_prev = C_HS1 if t % 2 == 0 else C_HS0

                def _scan(v, pg=pg, w=da_s, hc=hs_cur, hp=hs_prev, wy=prev_y_t, t=t):
                    v.wait_ge(s_s.h, w)
                    if wy:
                        v.wait_ge(s_t.h, wy)
                    v.tensor_copy(SC[:, C_DB:C_DB + 512], pg[0:16, 1024:1536])
                    v.tensor_tensor(SC[:, C_BXD:C_BXD + 512], pg[0:16, 1536:2048],
                                    SC[:, C_DB:C_DB + 512], ALU.mult)
                    init = 0.0 if t == 0 else SC[:, hp + 511:hp + 512]
                    v.tensor_tensor_scan(SC[:, hc:hc + 512], SC[:, C_DA:C_DA + 512],
                                         SC[:, C_BXD:C_BXD + 512], init, ALU.mult, ALU.add)
                    v.tensor_copy(HSB[:], SC[:, hc:hc + 512]).then_inc(s_v.h, 1)
                PH.v(_scan)
                s_v.n += 1
                scan_v = s_v.n
                grp_req[gi] = (s_v, scan_v)
            else:
                S_SEL1, S_SEL2, S_PREV1, S_CUR0, C_HS2 = 2048, 2049, 2050, 2051, 2064

                def _sc1(v, pg=pg, w=da_s, wy=prev_y_t, wso=s_so.n):
                    v.wait_ge(s_s.h, w)
                    if wy:
                        v.wait_ge(s_t.h, wy)
                    if wso:
                        v.wait_ge(s_so.h, wso)
                    v.tensor_copy(SC[:, C_DB:C_DB + 512], pg[0:16, 1024:1536])
                    v.tensor_tensor(SC[:, C_BXD:C_BXD + 512], pg[0:16, 1536:2048],
                                    SC[:, C_DB:C_DB + 512], ALU.mult)
                    v.tensor_scalar(SC[:, S_SEL1:S_SEL1 + 1], SC[:, S_PREV1:S_PREV1 + 1],
                                    WSEL[0:16, 0:1], None, ALU.mult)
                    v.tensor_tensor_scan(SC[:, C_HS0:C_HS0 + 512], SC[:, C_DA:C_DA + 512],
                                         SC[:, C_BXD:C_BXD + 512], SC[:, S_SEL1:S_SEL1 + 1],
                                         ALU.mult, ALU.add).then_inc(s_v.h, 1)
                PH.v(_sc1)
                s_v.n += 1
                scan1_v = s_v.n

                def _si1(g, wv=scan1_v, wcc=s_cc.n):
                    g.wait_ge(s_v.h, wv)
                    g.wait_ge(s_cc.h, wcc)
                    g.dma_start(out=cs_in,
                                in_=SC[:, C_HS0 + 511:C_HS0 + 512]).then_inc(s_si.h, 16)
                PH.g(_si1)
                s_si.n += 16

                def _agsa(g, wsi=s_si.n, wso=s_so.n):
                    g.wait_ge(s_si.h, wsi)
                    if wso:
                        g.wait_ge(s_so.h, wso)
                    g.collective_compute("AllGather", ALU.bypass, replica_groups=RG,
                                         ins=[cs_in], outs=[cs_out]).then_inc(s_cc.h, 1)
                PH.g(_agsa)
                s_cc.n += 1

                def _so1(g, wcc=s_cc.n, wpv=prev_scan2_v):
                    g.wait_ge(s_cc.h, wcc)
                    if wpv:
                        g.wait_ge(s_v.h, wpv)
                    g.dma_start(out=SC[:, S_CUR0:S_CUR0 + 1], in_=cs_out[0:1, :]).then_inc(s_so.h, 16)
                PH.g(_so1)
                s_so.n += 16
                so1_n = s_so.n

                grp_req[gi] = (s_v, scan1_v)
                emit_gu(range(4, 6))

                def _sc2(v, pg=pg, wso=so1_n):
                    v.wait_ge(s_so.h, wso)
                    v.tensor_scalar(SC[:, S_SEL2:S_SEL2 + 1], SC[:, S_CUR0:S_CUR0 + 1],
                                    WSEL[0:16, 1:2], None, ALU.mult)
                    v.tensor_tensor_scan(SC[:, C_HS2:C_HS2 + 512], SC[:, C_DA:C_DA + 512],
                                         SC[:, C_BXD:C_BXD + 512], SC[:, S_SEL2:S_SEL2 + 1],
                                         ALU.mult, ALU.add)
                    v.tensor_scalar(SC[:, C_HS0:C_HS0 + 512], SC[:, C_HS0:C_HS0 + 512],
                                    WSEL[0:16, 0:1], None, ALU.mult)
                    v.scalar_tensor_tensor(SC[:, C_HS0:C_HS0 + 512], SC[:, C_HS2:C_HS2 + 512],
                                           WSEL[0:16, 1:2], SC[:, C_HS0:C_HS0 + 512],
                                           ALU.mult, ALU.add)
                    v.tensor_copy(HSB[:], SC[:, C_HS0:C_HS0 + 512]).then_inc(s_v.h, 1)
                PH.v(_sc2)
                s_v.n += 1
                scan_v = s_v.n
                prev_scan2_v = scan_v

                if t < nt - 1:
                    def _si2(g, wv=scan_v, wcc=s_cc.n):
                        g.wait_ge(s_v.h, wv)
                        g.wait_ge(s_cc.h, wcc)
                        g.dma_start(out=cs_in, in_=SC[:, C_HS2 + 511:C_HS2 + 512]).then_inc(s_si.h, 16)
                    PH.g(_si2)
                    s_si.n += 16

                    def _agsb(g, wsi=s_si.n, wso=s_so.n):
                        g.wait_ge(s_si.h, wsi)
                        g.wait_ge(s_so.h, wso)
                        g.collective_compute("AllGather", ALU.bypass, replica_groups=RG,
                                             ins=[cs_in], outs=[cs_out]).then_inc(s_cc.h, 1)
                    PH.g(_agsb)
                    s_cc.n += 1

                    def _so2(g, wcc=s_cc.n, wv1=scan1_v):
                        g.wait_ge(s_cc.h, wcc)
                        g.wait_ge(s_v.h, wv1)
                        g.dma_start(out=SC[:, S_PREV1:S_PREV1 + 1], in_=cs_out[1:2, :]).then_inc(s_so.h, 16)
                    PH.g(_so2)
                    s_so.n += 16

            emit_gu(range(6, 12))
            cur_gu_t = s_t.n
            hh_v = s_v.n
            if t + 1 < nt:
                emit_front(t + 1, whh=hh_v)
            for w in range(4):
                gi, pg, req = next_group()

                def _mm(te, req=req, pg=pg, w=w, ws=scan_v):
                    if req is not None:
                        te.wait_ge(req[0].h, req[1])
                    te.wait_ge(s_v.h, ws)
                    i = None
                    for cl in range(4):
                        c = w * 4 + cl
                        te.matmul(pg[:, cl * 512:(cl + 1) * 512], CBF[:, c * 128:(c + 1) * 128],
                                  HSB[:], start=True, stop=False)
                        i = te.matmul(pg[:, cl * 512:(cl + 1) * 512], DBF[:, c * 128:(c + 1) * 128],
                                      XO[:, c * TT:(c + 1) * TT], start=False, stop=True)
                    i.then_inc(s_t.h, 1)
                PH.t(_mm)
                s_t.n += 1
                wv_t = s_t.n

                def _ev(v, pg=pg, w=w, wv=wv_t):
                    v.wait_ge(s_t.h, wv)
                    i = None
                    for cl in range(4):
                        c = w * 4 + cl
                        i = v.tensor_tensor(SPY[:, c * TT:(c + 1) * TT], pg[:, cl * 512:(cl + 1) * 512],
                                            GS[:, c * TT:(c + 1) * TT], ALU.mult)
                    i.then_inc(s_v.h, 1)
                PH.v(_ev)
                s_v.n += 1
                grp_req[gi] = (s_v, s_v.n)
            ybf_v = s_v.n
            prev_y_t_new = s_t.n
            if t + 1 < nt:
                emit_front2(t + 1, cur_gu_t)

            def _outrdy(te, r0=grp_req[0], r1=grp_req[1]):
                if r0 is not None:
                    te.wait_ge(r0[0].h, r0[1])
                if r1 is not None:
                    te.wait_ge(r1[0].h, r1[1])
            PH.t(_outrdy)

            for kg in range(4):
                buf, wn = emit_slab(wov[kg], "wo")

                def _mm(te, buf=buf, wn=wn, kg=kg):
                    te.wait_ge(wn[0].h, wn[1])
                    i = None
                    for kl in range(4):
                        k = kg * 4 + kl
                        for m in range(8):
                            pg = pgA if m < 4 else pgB
                            i = te.matmul(pg[:, (m % 4) * 512:(m % 4) * 512 + 512],
                                          buf[:, kl * 1024 + m * 128:kl * 1024 + m * 128 + 128],
                                          SPY[:, k * TT:(k + 1) * TT],
                                          start=(k == 0), stop=False)
                    i.then_inc(s_t.h, 1)
                PH.t(_mm)
                s_t.n += 1
                slab_done()
            for kg in range(6):
                buf, wn = emit_slab(dnv[kg], "dn")

                def _mm(te, buf=buf, wn=wn, kg=kg, wh=hh_v):
                    te.wait_ge(wn[0].h, wn[1])
                    if kg == 0:
                        te.wait_ge(s_v.h, wh)
                    i = None
                    for kl in range(4):
                        k = kg * 4 + kl
                        for m in range(8):
                            pg = pgA if m < 4 else pgB
                            i = te.matmul(pg[:, (m % 4) * 512:(m % 4) * 512 + 512],
                                          buf[:, kl * 1024 + m * 128:kl * 1024 + m * 128 + 128],
                                          HH[:, k * TT:(k + 1) * TT],
                                          start=False, stop=(k == 23))
                    i.then_inc(s_t.h, 1)
                PH.t(_mm)
                s_t.n += 1
                slab_done()
            out_t = s_t.n

            def _oevA(v, w=out_t, t=t):
                v.wait_ge(s_t.h, w)
                if t > 0:
                    v.wait_ge(s_o.h, 16 * t)
                i = None
                for m in range(4):
                    i = v.tensor_tensor(XOf[:, m * 512:(m + 1) * 512],
                                        pgA[:, m * 512:m * 512 + 512],
                                        XR[:, m * TT:(m + 1) * TT], ALU.add)
                i.then_inc(s_v.h, 1)
            PH.v(_oevA)
            s_v.n += 1
            out_vA = s_v.n

            def _oevB(v):
                i = None
                for m in range(4, 8):
                    i = v.tensor_tensor(XOf[:, m * 512:(m + 1) * 512],
                                        pgB[:, (m % 4) * 512:(m % 4) * 512 + 512],
                                        XR[:, m * TT:(m + 1) * TT], ALU.add)
                i.then_inc(s_v.h, 1)
            PH.v(_oevB)
            s_v.n += 1
            out_v = s_v.n
            grp_req[0] = (s_v, out_vA)
            grp_req[1] = (s_v, out_v)
            if t + 1 < nt:
                emit_r(t + 1)

            def _ost(g, t0=t0, w=out_v):
                g.wait_ge(s_v.h, w)
                dst = out_fm.rearrange("(c p) t -> p c t", p=128)[:, :, t0:t0 + TT]
                g.dma_start(out=dst, in_=XOf.rearrange("p (c t) -> p c t", c=DC)).then_inc(s_o.h, 16)
            PH.g(_ost)
            s_o.n += 16

            prev_ybf_v = ybf_v
            prev_win_t = prev_win_t_new
            prev_y_t = prev_y_t_new
            PH.flush()

        if debug:
            @block.gpsimd
            def _(g):
                g.wait_ge(s_o.h, s_o.n)
                for nm, tn in [("d_r", XR), ("d_xn", TB), ("d_hbf", HBF), ("d_xnbf", XNBF),
                               ("d_xspre", XSPRE), ("d_gs", GS), ("d_spy", SPY),
                               ("d_hh", HH), ("d_sc", SC), ("d_sta", STA)]:
                    s_o.inc(g.dma_start(out=dbg[nm], in_=tn[:]), 16)

        @block.gpsimd
        def _(g):
            g.wait_ge(s_o.h, s_o.n)

    return nc
_PROG_CACHE = {}


def _get_prog(nt):
    if nt not in _PROG_CACHE:
        _PROG_CACHE[nt] = build_program(nt, zigzag=True)
    return _PROG_CACHE[nt]


def _win_slabs(w_in):
    a = np.ascontiguousarray(w_in, np.float32).reshape(8, 128, 8, 512)
    return np.ascontiguousarray(a.transpose(2, 1, 0, 3).reshape(8 * 128, 4096))


def _wd_slabs(wd):
    a = np.ascontiguousarray(wd, np.float32).reshape(2, 8, 128, 4, 512)
    return np.ascontiguousarray(a.transpose(3, 0, 2, 1, 4).reshape(8 * 128, 4096))


def _wout_slabs(w_out):
    a = np.ascontiguousarray(w_out, np.float32).reshape(4, 4, 128, 1024)
    return np.ascontiguousarray(a.transpose(0, 2, 1, 3).reshape(4 * 128, 4096))


def _gu_slabs(w_gate, w_up):
    ag = np.ascontiguousarray(np.asarray(w_gate, np.int8).T).reshape(8, 128, 12, 256)
    au = np.ascontiguousarray(np.asarray(w_up, np.int8).T).reshape(8, 128, 12, 256)
    s = np.stack([ag, au], axis=3)
    return np.ascontiguousarray(s.transpose(2, 1, 0, 3, 4).reshape(12 * 128, 4096))


def _dn_slabs(w_down):
    a = np.ascontiguousarray(np.asarray(w_down, np.int8).T).reshape(6, 4, 128, 1024)
    return np.ascontiguousarray(a.transpose(0, 2, 1, 3).reshape(6 * 128, 4096))


def _host_prep(L0_attn_ln_gamma, L0_attn_ln_beta, L0_attn_W_in, L0_attn_conv_w,
               L0_attn_conv_b, L0_attn_A_log, L0_attn_B_mat, L0_attn_C_mat,
               L0_attn_D_vec, L0_attn_Wd, L0_attn_bd, L0_attn_W_out,
               L0_ffn_w_gate, L0_ffn_w_up, L0_ffn_w_down):
    cw = np.asarray(L0_attn_conv_w, np.float32).reshape(DI, 4)
    convd = np.zeros((4 * 128, DIC * 128), np.float32)
    for w in range(4):
        for cl in range(4):
            c = w * 4 + cl
            for k in range(4):
                blk = convd[w * 128:(w + 1) * 128, (cl * 4 + k) * 128:(cl * 4 + k + 1) * 128]
                np.fill_diagonal(blk, cw[c * 128:(c + 1) * 128, k])
    dvec = np.asarray(L0_attn_D_vec, np.float32)
    ddiag = np.zeros((128, DIC * 128), np.float32)
    for c in range(DIC):
        np.fill_diagonal(ddiag[:, c * 128:(c + 1) * 128], dvec[c * 128:(c + 1) * 128])
    return dict(
        win=_win_slabs(L0_attn_W_in),
        wd=_wd_slabs(L0_attn_Wd),
        wout=_wout_slabs(L0_attn_W_out),
        gus=_gu_slabs(L0_ffn_w_gate, L0_ffn_w_up),
        dns=_dn_slabs(L0_ffn_w_down),
        convd=convd, ddiag=ddiag,
        cpre=np.ascontiguousarray(np.asarray(L0_attn_C_mat, np.float32).T),
        bpre=np.ascontiguousarray(np.asarray(L0_attn_B_mat, np.float32).T),
        convb=np.ascontiguousarray(np.asarray(L0_attn_conv_b, np.float32).reshape(DIC, 128).T),
        bdb=np.ascontiguousarray(np.asarray(L0_attn_bd, np.float32).reshape(DIC, 128).T),
        gamv=np.ascontiguousarray(np.asarray(L0_attn_ln_gamma, np.float32).reshape(DC, 128).T),
        betv=np.ascontiguousarray(np.asarray(L0_attn_ln_beta, np.float32).reshape(DC, 128).T),
        alog=np.ascontiguousarray(np.asarray(L0_attn_A_log, np.float32).reshape(1, NS)),
        cvecd=np.repeat(np.array([[EPS, -0.5, -1.0, QS, 1.0 + EPS, EPS * EPS]], np.float32), 128, axis=0),
        wseld=np.zeros((128, 2), np.float32),
    )


def kernel(L0_attn_x, L0_attn_ln_gamma, L0_attn_ln_beta, L0_attn_W_in,
           L0_attn_conv_w, L0_attn_conv_b, L0_attn_A_log, L0_attn_B_mat,
           L0_attn_C_mat, L0_attn_D_vec, L0_attn_Wd, L0_attn_bd,
           L0_attn_W_out, L0_ffn_w_gate, L0_ffn_w_up, L0_ffn_w_down):
    x = np.asarray(L0_attn_x)
    B, L, _ = x.shape
    ntg = L // TT
    nt = ntg // 2

    shared = _host_prep(L0_attn_ln_gamma, L0_attn_ln_beta, L0_attn_W_in,
                        L0_attn_conv_w, L0_attn_conv_b, L0_attn_A_log,
                        L0_attn_B_mat, L0_attn_C_mat, L0_attn_D_vec,
                        L0_attn_Wd, L0_attn_bd, L0_attn_W_out,
                        L0_ffn_w_gate, L0_ffn_w_up, L0_ffn_w_down)

    in_maps = []
    for c in range(8):
        b, par = c // 2, c % 2
        m = dict(shared)
        xt = x[b % B].T.astype(np.float32).reshape(D, ntg, TT)
        m["x_fm"] = np.ascontiguousarray(xt[:, par::2, :].reshape(D, nt * TT))
        ws = np.zeros((128, 2), np.float32)
        ws[:, par] = 1.0
        m["wseld"] = ws
        in_maps.append(m)

    nc = _get_prog(nt)
    res = run_bass_kernel_spmd(nc, in_maps, core_ids=list(range(8)))
    out = np.empty_like(x)
    for b in range(B):
        ot = np.empty((D, ntg, TT), np.float32)
        ot[:, 0::2, :] = res.results[2 * b]["out_fm"].reshape(D, nt, TT)
        ot[:, 1::2, :] = res.results[2 * b + 1]["out_fm"].reshape(D, nt, TT)
        out[b] = ot.reshape(D, L).T
    return out
